# version 19
# baseline (speedup 1.0000x reference)
import contextlib
import numpy as np
import ml_dtypes
import concourse.bass as bass
import concourse.mybir as mybir
from concourse.bass_utils import run_bass_kernel_spmd

F32 = mybir.dt.float32
BF16 = mybir.dt.bfloat16
I32 = mybir.dt.int32
U32 = mybir.dt.uint32
AF = mybir.ActivationFunctionType
OP = mybir.AluOpType

D = 2048
L = 8192
NST = 16
H = 5632
NC = 8
DS = D // NC
TS = L // NC
CH = 1024
M = L // CH
KT = D // 128
HT = H // 128
DT_MIN = 1e-4
PRELOAD = False
LN_EPS = 1e-5


class Toks:

    def __init__(self, nc, stack):
        self.nc = nc
        self.stack = stack
        self.h = {}
        self.n = {}
        self.mark = {}

    def sem(self, name):
        if name not in self.h:
            self.h[name] = self.stack.enter_context(self.nc.semaphore(name))
            self.n[name] = 0
        return self.h[name]

    def tag(self, name, inst, amt=1):
        s = self.sem(name)
        inst.then_inc(s, amt)
        self.n[name] += amt
        return (name, self.n[name])

    def wait(self, eng, ename, *tokens):
        for t in tokens:
            if t is None:
                continue
            name, v = t
            if v <= 0:
                continue
            key = (ename, name)
            if self.mark.get(key, 0) >= v:
                continue
            eng.wait_ge(self.sem(name), v)
            self.mark[key] = v


def build_nc(debug=False):
    nc = bass.Bass("TRN2", target_bir_lowering=False)

    u_fm = nc.dram_tensor("u_fm", [D, L], F32, kind="ExternalInput")
    u_ds = nc.dram_tensor("u_ds", [DS, L], F32, kind="ExternalInput")
    u_tok = nc.dram_tensor("u_tok", [D, TS], F32, kind="ExternalInput")
    wdt = nc.dram_tensor("wdt", [D, DS], F32, kind="ExternalInput")
    wb = nc.dram_tensor("wb", [D, DS], F32, kind="ExternalInput")
    wc = nc.dram_tensor("wc", [D, DS], F32, kind="ExternalInput")
    bvec_d = nc.dram_tensor("bvec", [128, 6], F32, kind="ExternalInput")
    araw_d = nc.dram_tensor("araw", [128, 2 * NST], F32, kind="ExternalInput")
    bbase_d = nc.dram_tensor("bbase", [128, 2 * NST], F32, kind="ExternalInput")
    cbase_d = nc.dram_tensor("cbase", [128, 2 * NST], F32, kind="ExternalInput")
    dvec_d = nc.dram_tensor("dvec", [128, 2], F32, kind="ExternalInput")
    sg_d = nc.dram_tensor("sgv", [128, HT], F32, kind="ExternalInput")
    su_d = nc.dram_tensor("suv", [128, HT], F32, kind="ExternalInput")
    sd_d = nc.dram_tensor("sdv", [128, KT], F32, kind="ExternalInput")
    wg_d = nc.dram_tensor("wg_t", [HT * 128, KT * 128], I32, kind="ExternalInput")
    wu_d = nc.dram_tensor("wu_t", [HT * 128, KT * 128], I32, kind="ExternalInput")
    wd_d = nc.dram_tensor("wd_t", [KT * 128, HT * 128], I32, kind="ExternalInput")
    yidx_d = nc.dram_tensor("yidx", [128, KT], U32, kind="ExternalInput")
    ident_d = nc.dram_tensor("ident", [128, 128], BF16, kind="ExternalInput")
    out_d = nc.dram_tensor("out", [D, TS], F32, kind="ExternalOutput")

    if debug:
        dbg_ubf = nc.dram_tensor("dbg_ubf", [128, KT * CH], BF16, kind="ExternalOutput")
        dbg_dt = nc.dram_tensor("dbg_dt", [128, 2 * CH], BF16, kind="ExternalOutput")
        dbg_g = nc.dram_tensor("dbg_g", [128, 2 * CH], BF16, kind="ExternalOutput")
        dbg_y = nc.dram_tensor("dbg_y", [DS, L], BF16, kind="ExternalOutput")
        dbg_res = nc.dram_tensor("dbg_res", [128, KT * TS], F32, kind="ExternalOutput")
        dbg_x1 = nc.dram_tensor("dbg_x1", [128, KT * TS], BF16, kind="ExternalOutput")
        dbg_res2 = nc.dram_tensor("dbg_res2", [128, KT * TS], BF16, kind="ExternalOutput")
        dbg_sgv = nc.dram_tensor("dbg_sgv", [128, HT], F32, kind="ExternalOutput")
        dbg_sg = nc.dram_tensor("dbg_sg", [128, TS], F32, kind="ExternalOutput")
        dbg_hact = nc.dram_tensor("dbg_hact", [128, TS], BF16, kind="ExternalOutput")
        dbg_wg = nc.dram_tensor("dbg_wg", [128, KT * 128], BF16, kind="ExternalOutput")
    y_ag_in = nc.dram_tensor("y_ag_in", [M * DS, CH], BF16)
    y_all8 = nc.dram_tensor("y_all8", [M * NC * DS, CH], BF16, addr_space="Shared")

    st = contextlib.ExitStack()
    with st:
        tk = Toks(nc, st)
        blk = st.enter_context(nc.Block())

        bvec = st.enter_context(nc.sbuf_tensor("bvec_t", [128, 6], F32))
        at = st.enter_context(nc.sbuf_tensor("at", [128, 2 * NST], F32))
        cb = st.enter_context(nc.sbuf_tensor("cb", [128, 2 * NST], F32))
        dvec = st.enter_context(nc.sbuf_tensor("dvec_t", [128, 2], F32))
        sgv = st.enter_context(nc.sbuf_tensor("sgv_t", [128, HT], F32))
        suv = st.enter_context(nc.sbuf_tensor("suv_t", [128, HT], F32))
        sdv = st.enter_context(nc.sbuf_tensor("sdv_t", [128, KT], F32))
        yidx = st.enter_context(nc.sbuf_tensor("yidx_t", [128, KT], U32))
        onescol = st.enter_context(nc.sbuf_tensor("onescol", [128, 1], BF16))
        onesrow = st.enter_context(nc.sbuf_tensor("onesrow", [1, 128], F32))
        epsb = st.enter_context(nc.sbuf_tensor("epsb", [1, 1], F32))
        ident = st.enter_context(nc.sbuf_tensor("ident_t", [128, 128], BF16))

        def preload(scalar, set_id):
            return scalar.add_instruction(mybir.InstLoadActFuncSet(
                name=scalar.bass.get_next_instruction_name(),
                act_func_set_id=set_id, ins=[], outs=[]))

        sb = contextlib.ExitStack()
        with sb:
            wdt_bf = sb.enter_context(nc.sbuf_tensor("wdt_bf", [128, KT * DS], BF16))
            wb_bf = sb.enter_context(nc.sbuf_tensor("wb_bf", [128, KT * DS], BF16))
            wc_bf = sb.enter_context(nc.sbuf_tensor("wc_bf", [128, KT * DS], BF16))
            araw = sb.enter_context(nc.sbuf_tensor("araw_t", [128, 2 * NST], F32))
            bbase = sb.enter_context(nc.sbuf_tensor("bbase_t", [128, 2 * NST], F32))
            cbase = sb.enter_context(nc.sbuf_tensor("cbase_t", [128, 2 * NST], F32))
            ubf = [sb.enter_context(nc.sbuf_tensor(f"ubf{p}", [128, KT * CH], BF16)) for p in range(2)]
            usl = [sb.enter_context(nc.sbuf_tensor(f"usl{p}", [128, 2 * CH], BF16)) for p in range(2)]
            dt_sb = [sb.enter_context(nc.sbuf_tensor(f"dt{p}", [128, 2 * CH], BF16)) for p in range(2)]
            b_sb = sb.enter_context(nc.sbuf_tensor("b_sb", [128, 2 * CH], BF16))
            c_sb = [sb.enter_context(nc.sbuf_tensor(f"c{p}", [128, 2 * CH], BF16)) for p in range(2)]
            g_sb = [sb.enter_context(nc.sbuf_tensor(f"g{p}", [128, 2 * CH], BF16)) for p in range(2)]
            gtmp = sb.enter_context(nc.sbuf_tensor("gtmp", [128, CH], F32))
            abar = [sb.enter_context(nc.sbuf_tensor(f"abar{p}", [128, CH], BF16)) for p in range(8)]
            xbuf = [sb.enter_context(nc.sbuf_tensor(f"xbuf{p}", [128, CH], BF16)) for p in range(4)]
            cbd = sb.enter_context(nc.sbuf_tensor("cbd", [128, 2 * NST * 128], BF16))
            ytmp = sb.enter_context(nc.sbuf_tensor("ytmp", [128, CH], F32))
            y_out = [sb.enter_context(nc.sbuf_tensor(f"yout{p}", [128, 2 * CH], BF16)) for p in range(2)]
            stcols = sb.enter_context(nc.sbuf_tensor("stcols", [128, 2 * NST], BF16))
            sp_a = [sb.enter_context(nc.sbuf_tensor(f"spa{p}", [128, 512], F32)) for p in range(2)]
            sp_b = [sb.enter_context(nc.sbuf_tensor(f"spb{p}", [128, 512], F32)) for p in range(2)]
            rt = [sb.enter_context(nc.sbuf_tensor(f"rt{p}", [128, 512], F32)) for p in range(2)]
            pdt = [sb.enter_context(nc.psum_tensor(f"pdt{p}", [128, 512], F32)) for p in range(2)]
            pb = [sb.enter_context(nc.psum_tensor(f"pb{p}", [128, 512], F32)) for p in range(2)]
            pc = [sb.enter_context(nc.psum_tensor(f"pc{p}", [128, 512], F32)) for p in range(2)]
            pacc = [sb.enter_context(nc.psum_tensor(f"pacc{s}", [128, 512], F32)) for s in range(2)]

            def a_sync(sync):
                for dst, src in [(bvec, bvec_d), (araw, araw_d), (bbase, bbase_d),
                                 (cbase, cbase_d), (dvec, dvec_d), (sgv, sg_d),
                                 (suv, su_d), (sdv, sd_d), (yidx, yidx_d),
                                 (ident, ident_d)]:
                    tk.tag("dcon", sync.dma_start(out=dst[:], in_=src[:, :]), 16)
            blk.sync(a_sync)
            t_consts = ("dcon", tk.n["dcon"])

            def a_gp(gp):
                for w_bf, w_d in [(wdt_bf, wdt), (wb_bf, wb), (wc_bf, wc)]:
                    tk.tag("dwp", gp.dma_start(
                        out=w_bf[:].rearrange("p (k f) -> p k f", k=KT),
                        in_=w_d[:, :].rearrange("(k p) f -> p k f", k=KT)), 16)
            blk.gpsimd(a_gp)
            t_wload = ("dwp", tk.n["dwp"])

            def a_vec(vector):
                tk.wait(vector, "v", t_consts)
                t = tk.tag("vq", vector.tensor_scalar(out=sp_a[0][:, 0:2 * NST], in0=araw[:],
                                                      scalar1=0.0, scalar2=-1.0, op0=OP.max, op1=OP.mult))
                tk.tag("vq", vector.tensor_tensor(out=cb[:], in0=bbase[:], in1=cbase[:], op=OP.mult))
                tk.tag("vq", vector.memset(onescol[:], 1.0))
                tk.tag("vq", vector.memset(onesrow[:], 1.0))
                tk.tag("vq", vector.memset(epsb[:], LN_EPS))
                return t
            blk.vector(a_vec)
            t_rneg = ("vq", tk.n["vq"] - 3)

            def a_act(scalar):
                tk.wait(scalar, "s", t_consts)
                if PRELOAD:
                    preload(scalar, 6)
                tk.tag("aq", scalar.activation(out=sp_b[0][:, 0:2 * NST], in_=araw[:], func=AF.Abs))
                tk.wait(scalar, "s", ("aq", tk.n["aq"]))
                tk.tag("aq", scalar.activation(out=sp_b[0][:, 0:2 * NST], in_=sp_b[0][:, 0:2 * NST],
                                               func=AF.Exp, scale=-1.0))
                tk.wait(scalar, "s", ("aq", tk.n["aq"]))
                tk.tag("aq", scalar.activation(out=sp_b[0][:, 0:2 * NST], in_=sp_b[0][:, 0:2 * NST],
                                               func=AF.Ln, bias=1.0))
            blk.scalar(a_act)
            t_ln1p = ("aq", tk.n["aq"])

            def a_vec2(vector):
                tk.wait(vector, "v", t_ln1p, t_rneg)
                tk.tag("vq", vector.scalar_tensor_tensor(out=at[:], in0=sp_b[0][:, 0:2 * NST],
                                                         scalar=-1.0, in1=sp_a[0][:, 0:2 * NST],
                                                         op0=OP.mult, op1=OP.add))
                for col in range(2 * NST):
                    tk.tag("vq", vector.tensor_scalar(
                        out=cbd[:, col * 128:(col + 1) * 128], in0=ident[:],
                        scalar1=cb[:, col:col + 1], scalar2=None, op0=OP.mult))
            blk.vector(a_vec2)
            t_at = ("vq", tk.n["vq"])
            t_cb = t_at

            t_ucast = [None, None]
            t_uslc = [None, None]
            t_proj_done = [None, None]
            t_epi_psum = [None] * 4
            t_epi_act = [None] * 4
            t_dt = {}
            t_bb = {}
            t_cc = {}
            t_g = {}
            t_scan = {}
            t_red = {}
            t_stcol = {}
            t_abar = {}
            t_ydma = [None, None]
            t_ystt = [None, None]
            t_c_used = [None, None]
            t_dt_used = [None, None]
            t_gu = {}
            t_usl_used = [None, None]

            t_yacc = [None]
            t_epi_mm_all = {}

            def emit_bgp(m):
                par = m % 2

                def b_gp(gp, m=m, par=par):
                    tk.wait(gp, "g", t_proj_done[par])
                    t_ucast[par] = tk.tag(f"du{par}", gp.dma_start(
                        out=ubf[par][:].rearrange("p (k t) -> p k t", k=KT),
                        in_=u_fm[:, m * CH:(m + 1) * CH].rearrange("(k p) t -> p k t", k=KT)), 16)
                    tk.wait(gp, "g", t_usl_used[par])
                    t_uslc[par] = tk.tag(f"dsl{par}", gp.dma_start(
                        out=usl[par][:].rearrange("p (j t) -> p j t", j=2),
                        in_=u_ds[:, m * CH:(m + 1) * CH].rearrange("(j p) t -> p j t", j=2)), 16)
                blk.gpsimd(b_gp)

            def emit_proj_js(m, js):
                par = m % 2
                j, s = js >> 1, js & 1
                p2 = js % 2

                def te_body(tensor, m=m, par=par, js=js, j=j, s=s, p2=p2):
                    tk.wait(tensor, "t", t_ucast[par], t_wload,
                            t_epi_psum[p2], t_epi_act[p2])
                    last = None
                    for k in range(KT):
                        rhs = ubf[par][:, k * CH + s * 512: k * CH + (s + 1) * 512]
                        tensor.matmul(out=pdt[p2][:, :],
                                      lhsT=wdt_bf[:, k * DS + j * 128: k * DS + (j + 1) * 128],
                                      rhs=rhs, start=(k == 0), stop=(k == KT - 1))
                        tensor.matmul(out=pb[p2][:, :],
                                      lhsT=wb_bf[:, k * DS + j * 128: k * DS + (j + 1) * 128],
                                      rhs=rhs, start=(k == 0), stop=(k == KT - 1))
                        last = tensor.matmul(out=pc[p2][:, :],
                                             lhsT=wc_bf[:, k * DS + j * 128: k * DS + (j + 1) * 128],
                                             rhs=rhs, start=(k == 0), stop=(k == KT - 1))
                    t_epi_mm_all[(m, js)] = tk.tag("tq", last)
                blk.tensor(te_body)

                def act_epi(scalar, m=m, par=par, js=js, j=j, s=s, p2=p2):
                    tk.wait(scalar, "s", t_epi_mm_all[(m, js)], t_at, t_epi_act[p2])
                    tk.tag("aq", scalar.activation(out=sp_a[p2][:], in_=pdt[p2][:, :],
                                                   func=AF.Abs, bias=bvec[:, j:j + 1]))
                    tk.wait(scalar, "s", ("aq", tk.n["aq"]))
                    tk.tag("aq", scalar.activation(out=sp_b[p2][:], in_=sp_a[p2][:],
                                                   func=AF.Exp, scale=-1.0))
                    tk.wait(scalar, "s", ("aq", tk.n["aq"]))
                    t_epi_act[p2] = tk.tag("aq", scalar.activation(out=sp_b[p2][:], in_=sp_b[p2][:],
                                                                   func=AF.Ln, bias=1.0))
                blk.scalar(act_epi)
                t_lnjs = ("aq", tk.n["aq"])

                def vec_epi(vector, m=m, par=par, js=js, j=j, s=s, p2=p2, t_lnjs=t_lnjs):
                    tk.wait(vector, "v", t_epi_mm_all[(m, js)], t_epi_psum[p2],
                            t_g.get((m - 1, j)))
                    tk.tag("vq", vector.tensor_scalar(out=rt[p2][:], in0=pdt[p2][:, :],
                                                      scalar1=bvec[:, j:j + 1], scalar2=0.0,
                                                      op0=OP.add, op1=OP.max))
                    tk.wait(vector, "v", t_dt_used[par], t_lnjs, ("vq", tk.n["vq"]))
                    tk.tag("vq", vector.scalar_tensor_tensor(
                        out=dt_sb[par][:, j * CH + s * 512: j * CH + (s + 1) * 512],
                        in0=rt[p2][:], scalar=DT_MIN, in1=sp_b[p2][:], op0=OP.add, op1=OP.add))
                    t_dt[(m, j, s)] = ("vq", tk.n["vq"])
                    tk.tag("vq", vector.tensor_scalar(
                        out=b_sb[:, j * CH + s * 512: j * CH + (s + 1) * 512],
                        in0=pb[p2][:, :], scalar1=bvec[:, 2 + j:3 + j], scalar2=None, op0=OP.add))
                    t_bb[(m, j, s)] = ("vq", tk.n["vq"])
                    tk.wait(vector, "v", t_c_used[par])
                    t_epi_psum[p2] = tk.tag("vq", vector.tensor_scalar(
                        out=c_sb[par][:, j * CH + s * 512: j * CH + (s + 1) * 512],
                        in0=pc[p2][:, :], scalar1=bvec[:, 4 + j:5 + j], scalar2=None, op0=OP.add))
                    t_cc[(m, j, s)] = t_epi_psum[p2]
                blk.vector(vec_epi)

                if s == 1:
                    def vec_g(vector, m=m, par=par, j=j):
                        tk.wait(vector, "v", t_dt[(m, j, 0)], t_dt[(m, j, 1)],
                                t_bb[(m, j, 0)], t_bb[(m, j, 1)], t_uslc[par])
                        tk.tag("vq", vector.tensor_tensor(
                            out=gtmp[:], in0=dt_sb[par][:, j * CH:(j + 1) * CH],
                            in1=b_sb[:, j * CH:(j + 1) * CH], op=OP.mult))
                        tk.wait(vector, "v", t_gu.get((par, j)), ("vq", tk.n["vq"]))
                        tk.tag("vq", vector.tensor_tensor(
                            out=g_sb[par][:, j * CH:(j + 1) * CH], in0=gtmp[:],
                            in1=usl[par][:, j * CH:(j + 1) * CH], op=OP.mult))
                        t_g[(m, j)] = ("vq", tk.n["vq"])
                    blk.vector(vec_g)
                if js == 3:
                    t_proj_done[par] = t_epi_mm_all[(m, 3)]

            def prev_nk(m, j, n):
                if n >= 2:
                    return (m, j, n - 2)
                if m >= 1:
                    return (m - 1, j, NST - 2 + n)
                return None

            def prev_ab(m, j, n):
                if n >= 4:
                    return (m, j, n - 4)
                if m >= 1:
                    return (m - 1, j, NST - 4 + n)
                return None

            def emit_scan_n(m, j, n):
                par = m % 2
                q = j * 2 + (n & 1)
                pk = prev_nk(m, j, n)

                if n % 4 == 0:
                    def act_abar4(scalar, m=m, par=par, j=j, n0=n):
                        tk.wait(scalar, "s", t_dt[(m, j, 0)], t_dt[(m, j, 1)])
                        for nn in range(n0, n0 + 4):
                            qa = j * 4 + nn % 4
                            pka = prev_ab(m, j, nn)
                            tk.wait(scalar, "s", t_scan.get(pka), t_abar.get(pka))
                            t_abar[(m, j, nn)] = tk.tag("aq", scalar.activation(
                                out=abar[qa][:], in_=dt_sb[par][:, j * CH:(j + 1) * CH],
                                func=AF.Exp, scale=at[:, j * NST + nn:j * NST + nn + 1]))
                    blk.scalar(act_abar4)

                def vec_scan_n(vector, m=m, par=par, j=j, n=n, q=q, pk=pk):
                    tk.wait(vector, "v", t_abar[(m, j, n)], t_g[(m, j)],
                            t_red.get(pk), t_at,
                            t_stcol.get((m - 1, j, n)))
                    init = 0.0 if m == 0 else stcols[:, j * NST + n:j * NST + n + 1]
                    tk.tag("vq", vector.tensor_tensor_scan(
                        out=xbuf[q][:], data0=abar[j * 4 + n % 4][:],
                        data1=g_sb[par][:, j * CH:(j + 1) * CH],
                        initial=init, op0=OP.mult, op1=OP.add))
                    t_scan[(m, j, n)] = ("vq", tk.n["vq"])
                    tk.wait(vector, "v", t_scan[(m, j, n)])
                    tk.tag("vq", vector.tensor_copy(
                        out=stcols[:, j * NST + n:j * NST + n + 1],
                        in_=xbuf[q][:, CH - 1:CH]))
                    t_stcol[(m, j, n)] = ("vq", tk.n["vq"])
                blk.vector(vec_scan_n)

                def te_red_n(tensor, m=m, par=par, j=j, n=n, q=q):
                    tk.wait(tensor, "t", t_scan[(m, j, n)], t_at)
                    if n == 0:
                        tk.wait(tensor, "t", t_yacc[0])
                    last = None
                    for s in range(2):
                        last = tensor.matmul(
                            out=pacc[s][:, :],
                            lhsT=cbd[:, (j * NST + n) * 128:(j * NST + n + 1) * 128],
                            rhs=xbuf[q][:, s * 512:(s + 1) * 512],
                            start=(n == 0), stop=(n == NST - 1))
                    t_red[(m, j, n)] = tk.tag("tq", last)
                blk.tensor(te_red_n)

                if n == NST - 1:
                    t_gu[(par, j)] = t_scan[(m, j, n)]
                    if j == 1:
                        t_dt_used[par] = t_abar[(m, j, n)]

            def emit_y(m, j):
                par = m % 2

                def vec_y(vector, m=m, par=par, j=j):
                    tk.wait(vector, "v", t_red[(m, j, NST - 1)],
                            t_cc[(m, j, 0)], t_cc[(m, j, 1)], t_uslc[par], t_ydma[par])
                    for s in range(2):
                        tk.tag("vq", vector.tensor_tensor(
                            out=ytmp[:, s * 512:(s + 1) * 512], in0=pacc[s][:, :],
                            in1=c_sb[par][:, j * CH + s * 512: j * CH + (s + 1) * 512],
                            op=OP.mult))
                    t_yacc[0] = ("vq", tk.n["vq"])
                    tk.wait(vector, "v", t_yacc[0])
                    for s in range(2):
                        tk.tag("vq", vector.scalar_tensor_tensor(
                            out=y_out[par][:, j * CH + s * 512: j * CH + (s + 1) * 512],
                            in0=usl[par][:, j * CH + s * 512: j * CH + (s + 1) * 512],
                            scalar=dvec[:, j:j + 1],
                            in1=ytmp[:, s * 512:(s + 1) * 512], op0=OP.mult, op1=OP.add))
                    t_ystt[par] = ("vq", tk.n["vq"])
                    if j == 1:
                        t_usl_used[par] = t_ystt[par]
                        t_c_used[par] = t_ystt[par]
                blk.vector(vec_y)

            def emit_ysync(m):
                par = m % 2

                def b_sync(sync, m=m, par=par):
                    tk.wait(sync, "y", t_ystt[par])
                    t_ydma[par] = tk.tag(f"dy{par}", sync.dma_start(
                        out=y_ag_in[m * DS:(m + 1) * DS, :].rearrange("(j p) t -> p j t", j=2),
                        in_=y_out[par][:].rearrange("p (j t) -> p j t", j=2)), 16)
                    if debug:
                        t_ydma[par] = tk.tag(f"dy{par}", sync.dma_start(
                            out=dbg_y[:, m * CH:(m + 1) * CH].rearrange("(j p) t -> p j t", j=2),
                            in_=y_out[par][:].rearrange("p (j t) -> p j t", j=2)), 16)
                blk.sync(b_sync)

                def b_ag(gp, m=m, par=par):
                    tk.wait(gp, "g", t_ydma[par])
                    tk.tag("cs", gp.collective_compute(
                        "AllGather", OP.bypass, replica_groups=[list(range(NC))],
                        ins=[y_ag_in[m * DS:(m + 1) * DS, :].opt()],
                        outs=[y_all8[m * NC * DS:(m + 1) * NC * DS, :].opt()]))
                blk.gpsimd(b_ag)

            emit_bgp(0)
            for js in range(4):
                emit_proj_js(0, js)
            for m in range(1, M):
                emit_bgp(m)
                emit_proj_js(m, 0)
                emit_proj_js(m, 1)
                for n in range(NST):
                    emit_scan_n(m - 1, 0, n)
                emit_y(m - 1, 0)
                emit_proj_js(m, 2)
                emit_proj_js(m, 3)
                for n in range(NST):
                    emit_scan_n(m - 1, 1, n)
                emit_y(m - 1, 1)
                emit_ysync(m - 1)
            for n in range(NST):
                emit_scan_n(M - 1, 0, n)
            emit_y(M - 1, 0)
            for n in range(NST):
                emit_scan_n(M - 1, 1, n)
            emit_y(M - 1, 1)
            emit_ysync(M - 1)

            t_ylast0 = ("dy0", tk.n["dy0"])
            t_ylast1 = ("dy1", tk.n["dy1"])

        t_ag = ("cs", tk.n["cs"])
        t_b_done = [t_ylast0, t_ylast1, ("tq", tk.n["tq"]), ("aq", tk.n["aq"]),
                    ("vq", tk.n["vq"])]

        sdst = contextlib.ExitStack()
        with sdst:
            x1_bf = sdst.enter_context(nc.sbuf_tensor("x1bf", [128, KT * TS], BF16))
            wg_raw = [sdst.enter_context(nc.sbuf_tensor(f"wgr{p}", [128, KT * 128], BF16)) for p in range(3)]
            wu_raw = [sdst.enter_context(nc.sbuf_tensor(f"wur{p}", [128, KT * 128], BF16)) for p in range(3)]

            t_wcast = [None, None, None]
            t_wdq3 = [None, None, None]
            t_gmm3 = [None, None, None]
            t_wdc = [None, None]
            t_wddq2 = [None, None]
            t_dte = [None, None]

            def f_wload(i):
                p3 = i % 3

                def body(gp, i=i, p3=p3):
                    tk.wait(gp, "g", t_gmm3[p3], t_wdq3[p3])
                    if i < 3:
                        tk.wait(gp, "g", *t_b_done)
                    tk.tag(f"dwc{p3}", gp.dma_start(
                        out=wg_raw[p3][:],
                        in_=wg_d[i * 128:(i + 1) * 128, :]), 16)
                    t_wcast[p3] = tk.tag(f"dwc{p3}", gp.dma_start(
                        out=wu_raw[p3][:],
                        in_=wu_d[i * 128:(i + 1) * 128, :]), 16)
                blk.gpsimd(body)


            for i in range(3):
                f_wload(i)

            ln1 = contextlib.ExitStack()
            with ln1:
                res_sb = ln1.enter_context(nc.sbuf_tensor("res_sb", [128, KT * TS], F32))
                ut = [ln1.enter_context(nc.sbuf_tensor(f"ut{p}", [128, TS], F32)) for p in range(2)]
                ybf = [ln1.enter_context(nc.sbuf_tensor(f"ybf{p}", [128, TS], BF16)) for p in range(2)]
                rbf = [ln1.enter_context(nc.sbuf_tensor(f"rbf{p}", [128, TS], BF16)) for p in range(2)]
                sqbf = [ln1.enter_context(nc.sbuf_tensor(f"sqbf{p}", [128, TS], BF16)) for p in range(2)]
                srow = ln1.enter_context(nc.sbuf_tensor("srow", [1, 3 * TS], F32))
                ps_s1 = [ln1.enter_context(nc.psum_tensor(f"ps1{s}", [1, 512], F32)) for s in range(2)]
                ps_s2 = [ln1.enter_context(nc.psum_tensor(f"ps2{s}", [1, 512], F32)) for s in range(2)]
                ps_r = [ln1.enter_context(nc.psum_tensor(f"psr{s}", [128, 512], F32)) for s in range(2)]
                ps_q = [ln1.enter_context(nc.psum_tensor(f"psq{s}", [128, 512], F32)) for s in range(2)]
                xtmp = ln1.enter_context(nc.sbuf_tensor("xtmp", [128, 512], F32))

                t_ut = [None, None]
                t_yb = [None, None]
                t_res = {}
                t_rbf = {}
                t_sq = {}
                t_resk_used = [None, None]
                t_stat_mm = None

                for k in range(KT):
                    p = k % 2

                    def d_sync(sync, k=k, p=p):
                        tk.wait(sync, "y", t_resk_used[p])
                        t_ut[p] = tk.tag(f"dut{p}", sync.dma_start(
                            out=ut[p][:], in_=u_tok[k * 128:(k + 1) * 128, :]), 16)
                    blk.sync(d_sync)

                    def d_gp(gp, k=k, p=p):
                        tk.wait(gp, "g", t_ag, t_resk_used[p], t_consts)
                        t_yb[p] = tk.tag(f"dyb{p}", gp.indirect_dma_start(
                            out=ybf[p][:], out_offset=None,
                            in_=y_all8.ap(),
                            in_offset=bass.IndirectOffsetOnAxis(ap=yidx[:, k:k + 1], axis=0)), 16)
                    blk.gpsimd(d_gp)

                    def d_vec(vector, k=k, p=p):
                        tk.wait(vector, "v", t_ut[p], t_yb[p], t_resk_used[p])
                        tk.tag("vq", vector.tensor_tensor(
                            out=res_sb[:, k * TS:(k + 1) * TS], in0=ut[p][:], in1=ybf[p][:], op=OP.add))
                        t_res[k] = ("vq", tk.n["vq"])
                        tk.wait(vector, "v", t_res[k])
                        tk.tag("vq", vector.tensor_copy(out=rbf[p][:], in_=res_sb[:, k * TS:(k + 1) * TS]))
                        t_rbf[k] = ("vq", tk.n["vq"])
                    blk.vector(d_vec)

                    def d_act(scalar, k=k, p=p):
                        tk.wait(scalar, "s", t_res[k], t_resk_used[p], t_sq.get(k - 2))
                        tk.tag("aq", scalar.activation(out=sqbf[p][:], in_=res_sb[:, k * TS:(k + 1) * TS],
                                                       func=AF.Square))
                        t_sq[k] = ("aq", tk.n["aq"])
                    blk.scalar(d_act)

                    def d_te(tensor, k=k, p=p):
                        tk.wait(tensor, "t", t_rbf[k], t_sq[k])
                        last = None
                        for s in range(2):
                            tensor.matmul(out=ps_s1[s][:, :], lhsT=onescol[:, 0:1],
                                          rhs=rbf[p][:, s * 512:(s + 1) * 512],
                                          start=(k == 0), stop=(k == KT - 1))
                            last = tensor.matmul(out=ps_s2[s][:, :], lhsT=onescol[:, 0:1],
                                                 rhs=sqbf[p][:, s * 512:(s + 1) * 512],
                                                 start=(k == 0), stop=(k == KT - 1))
                        t_resk_used[p] = tk.tag("tq", last)
                    blk.tensor(d_te)
                t_stat_mm = t_resk_used[(KT - 1) % 2]

                def d_stats_vec(vector):
                    tk.wait(vector, "v", t_stat_mm)
                    for s in range(2):
                        tk.tag("vq", vector.tensor_scalar(out=srow[:, s * 512:(s + 1) * 512],
                                                          in0=ps_s1[s][:, :], scalar1=1.0 / D,
                                                          scalar2=None, op0=OP.mult))
                        tk.tag("vq", vector.tensor_scalar(out=srow[:, TS + s * 512:TS + (s + 1) * 512],
                                                          in0=ps_s2[s][:, :], scalar1=1.0 / D,
                                                          scalar2=None, op0=OP.mult))
                blk.vector(d_stats_vec)
                t_mrow = ("vq", tk.n["vq"])

                def d_stats_act(scalar):
                    tk.wait(scalar, "s", t_mrow)
                    tk.tag("aq", scalar.activation(out=srow[:, 2 * TS:3 * TS], in_=srow[:, 0:TS],
                                                   func=AF.Square))
                blk.scalar(d_stats_act)
                t_msq = ("aq", tk.n["aq"])

                def d_stats_vec2(vector):
                    tk.wait(vector, "v", t_msq)
                    tk.tag("vq", vector.tensor_tensor(out=srow[:, TS:2 * TS], in0=srow[:, TS:2 * TS],
                                                      in1=srow[:, 2 * TS:3 * TS], op=OP.subtract))
                blk.vector(d_stats_vec2)
                t_var = ("vq", tk.n["vq"])

                def d_stats_act2(scalar):
                    tk.wait(scalar, "s", t_var)
                    tk.tag("aq", scalar.activation(out=srow[:, TS:2 * TS], in_=srow[:, TS:2 * TS],
                                                   func=AF.Ln, bias=epsb[0:1, 0:1]))
                    tk.wait(scalar, "s", ("aq", tk.n["aq"]))
                    tk.tag("aq", scalar.activation(out=srow[:, TS:2 * TS], in_=srow[:, TS:2 * TS],
                                                   func=AF.Exp, scale=-0.5))
                blk.scalar(d_stats_act2)
                t_rstd = ("aq", tk.n["aq"])

                def d_stats_vec3(vector):
                    tk.wait(vector, "v", t_rstd)
                    tk.tag("vq", vector.tensor_tensor(out=srow[:, 0:TS], in0=srow[:, 0:TS],
                                                      in1=srow[:, TS:2 * TS], op=OP.mult))
                blk.vector(d_stats_vec3)
                t_qrow = ("vq", tk.n["vq"])

                def d_bcast(tensor):
                    tk.wait(tensor, "t", t_qrow)
                    last = None
                    for s in range(2):
                        tensor.matmul(out=ps_r[s][:, :], lhsT=onesrow[0:1, :],
                                      rhs=srow[0:1, TS + s * 512:TS + (s + 1) * 512],
                                      start=True, stop=True)
                        last = tensor.matmul(out=ps_q[s][:, :], lhsT=onesrow[0:1, :],
                                             rhs=srow[0:1, s * 512:(s + 1) * 512],
                                             start=True, stop=True)
                    tk.tag("tq", last)
                blk.tensor(d_bcast)
                t_bc = ("tq", tk.n["tq"])

                def d_x1(vector):
                    tk.wait(vector, "v", t_bc)
                    for k in range(KT):
                        for s in range(2):
                            tk.wait(vector, "v", ("vq", tk.n["vq"]))
                            tk.tag("vq", vector.tensor_tensor(
                                out=xtmp[:], in0=res_sb[:, k * TS + s * 512: k * TS + (s + 1) * 512],
                                in1=ps_r[s][:, :], op=OP.mult))
                            tk.wait(vector, "v", ("vq", tk.n["vq"]))
                            tk.tag("vq", vector.tensor_tensor(
                                out=x1_bf[:, k * TS + s * 512: k * TS + (s + 1) * 512],
                                in0=xtmp[:], in1=ps_q[s][:, :], op=OP.subtract))
                blk.vector(d_x1)
                t_x1 = ("vq", tk.n["vq"])

                if debug:
                    def dbg_ln1(sync):
                        tk.wait(sync, "y", t_x1)
                        tk.tag("ddbg", sync.dma_start(out=dbg_res[:, :], in_=res_sb[:]), 16)
                        tk.tag("ddbg", sync.dma_start(out=dbg_x1[:, :], in_=x1_bf[:]), 16)
                    blk.sync(dbg_ln1)

            ffn = contextlib.ExitStack()
            with ffn:
                hact = sdst.enter_context(nc.sbuf_tensor("hact", [128, HT * TS], BF16))
                sg_act = [ffn.enter_context(nc.sbuf_tensor(f"sga{p}", [128, TS], F32)) for p in range(2)]
                pg = [[ffn.enter_context(nc.psum_tensor(f"pg{p}{s}", [128, 512], F32)) for s in range(2)] for p in range(2)]
                pu = [[ffn.enter_context(nc.psum_tensor(f"pu{p}{s}", [128, 512], F32)) for s in range(2)] for p in range(2)]

                t_gmm = [None, None]
                t_hact = [None, None]
                t_silu = [None, None]

                def f_dq(i):
                    p3 = i % 3

                    def body(gp, i=i, p3=p3):
                        tk.wait(gp, "g", t_wcast[p3], t_gmm3[p3])
                        tk.tag("gq", gp.tensor_scalar(out=wg_raw[p3][:], in0=wg_raw[p3][:],
                                                      scalar1=-128.0, scalar2=1.0,
                                                      op0=OP.add, op1=OP.mult))
                        tk.tag("gq", gp.tensor_scalar(out=wu_raw[p3][:], in0=wu_raw[p3][:],
                                                      scalar1=-128.0, scalar2=1.0,
                                                      op0=OP.add, op1=OP.mult))
                        t_wdq3[p3] = ("gq", tk.n["gq"])
                    blk.gpsimd(body)

                f_dq(0)
                f_dq(1)
                for i in range(HT):
                    p = i % 2
                    p3 = i % 3

                    if i + 2 < HT:
                        f_dq(i + 2)

                    def f_te(tensor, i=i, p=p, p3=p3):
                        tk.wait(tensor, "t", t_wdq3[p3], t_x1, t_silu[p], t_hact[p])
                        last = None
                        for s in range(2):
                            for k in range(KT):
                                tensor.matmul(out=pg[p][s][:, :],
                                              lhsT=wg_raw[p3][:, k * 128:(k + 1) * 128],
                                              rhs=x1_bf[:, k * TS + s * 512: k * TS + (s + 1) * 512],
                                              start=(k == 0), stop=(k == KT - 1))
                            for k in range(KT):
                                last = tensor.matmul(out=pu[p][s][:, :],
                                                     lhsT=wu_raw[p3][:, k * 128:(k + 1) * 128],
                                                     rhs=x1_bf[:, k * TS + s * 512: k * TS + (s + 1) * 512],
                                                     start=(k == 0), stop=(k == KT - 1))
                        t_gmm[p] = tk.tag("tq", last)
                        t_gmm3[p3] = t_gmm[p]
                    blk.tensor(f_te)

                    def f_silu(scalar, i=i, p=p):
                        tk.wait(scalar, "s", t_gmm[p], t_consts, t_hact[p], t_silu[p])
                        if i == 0 and PRELOAD:
                            preload(scalar, 18)
                        last = None
                        for s in range(2):
                            last = scalar.activation(out=sg_act[p][:, s * 512:(s + 1) * 512],
                                                     in_=pg[p][s][:, :], func=AF.Silu,
                                                     scale=sgv[:, i:i + 1])
                        t_silu[p] = tk.tag("aq", last)
                    blk.scalar(f_silu)

                    if debug and i == 0:
                        def f_dbg(sync, p=p):
                            tk.wait(sync, "y", t_silu[p], t_wdq3[p % 3])
                            tk.tag("ddbg", sync.dma_start(out=dbg_sgv[:, :], in_=sgv[:]), 16)
                            tk.tag("ddbg", sync.dma_start(out=dbg_sg[:, :], in_=sg_act[p][:]), 16)
                            tk.tag("ddbg", sync.dma_start(out=dbg_wg[:, :], in_=wg_raw[p % 3][:]), 16)
                        blk.sync(f_dbg)

                    def f_hact(vector, i=i, p=p):
                        tk.wait(vector, "v", t_silu[p])
                        last = None
                        for s in range(2):
                            last = vector.scalar_tensor_tensor(
                                out=hact[:, i * TS + s * 512: i * TS + (s + 1) * 512],
                                in0=pu[p][s][:, :], scalar=suv[:, i:i + 1],
                                in1=sg_act[p][:, s * 512:(s + 1) * 512], op0=OP.mult, op1=OP.mult)
                        t_hact[p] = tk.tag("vq", last)
                    blk.vector(f_hact)

                    if i + 3 < HT:
                        f_wload(i + 3)

                if debug:
                    def f_dbg2(sync):
                        tk.wait(sync, "y", t_hact[0], t_hact[1])
                        tk.tag("ddbg", sync.dma_start(out=dbg_hact[:, :], in_=hact[:, 0:TS]), 16)
                    blk.sync(f_dbg2)

                t_hact_all = [t_hact[0], t_hact[1]]

            down = contextlib.ExitStack()
            with down:
                res2_sb = sdst.enter_context(nc.sbuf_tensor("res2", [128, KT * TS], BF16))
                wd_raw = [down.enter_context(nc.sbuf_tensor(f"wdr{p}", [128, HT * 128], BF16)) for p in range(2)]
                pd = [[down.enter_context(nc.psum_tensor(f"pd{p}{s}", [128, 512], F32)) for s in range(2)] for p in range(2)]

                t_dmm = [None, None]
                t_res2 = [None, None]

                def dn_wload(jd):
                    p2 = jd % 2

                    def body(gp, jd=jd, p2=p2):
                        tk.wait(gp, "g", t_dte[p2], t_wddq2[p2])
                        if jd < 2:
                            tk.wait(gp, "g", t_gmm[0], t_gmm[1], t_hact[0], t_hact[1],
                                    t_silu[0], t_silu[1])
                        t_wdc[p2] = tk.tag(f"dwdc{p2}", gp.dma_start(
                            out=wd_raw[p2][:],
                            in_=wd_d[jd * 128:(jd + 1) * 128, :]), 16)
                    blk.gpsimd(body)

                def dn_dq(jd):
                    p2 = jd % 2

                    def body(gp, jd=jd, p2=p2):
                        tk.wait(gp, "g", t_wdc[p2], t_dte[p2])
                        tk.tag("gq", gp.tensor_scalar(out=wd_raw[p2][:], in0=wd_raw[p2][:],
                                                      scalar1=-128.0, scalar2=1.0,
                                                      op0=OP.add, op1=OP.mult))
                        t_wddq2[p2] = ("gq", tk.n["gq"])
                    blk.gpsimd(body)

                dn_wload(0)
                dn_wload(1)
                dn_dq(0)
                dn_dq(1)

                for jd in range(KT):
                    p = jd % 2

                    def dn_te(tensor, jd=jd, p=p):
                        tk.wait(tensor, "t", t_wddq2[p], t_hact_all[0], t_hact_all[1],
                                t_res2[p])
                        last = None
                        for s in range(2):
                            for kh in range(HT):
                                last = tensor.matmul(out=pd[p][s][:, :],
                                                     lhsT=wd_raw[p][:, kh * 128:(kh + 1) * 128],
                                                     rhs=hact[:, kh * TS + s * 512: kh * TS + (s + 1) * 512],
                                                     start=(kh == 0), stop=(kh == HT - 1))
                        t_dmm[p] = tk.tag("tq", last)
                        t_dte[p] = t_dmm[p]
                    blk.tensor(dn_te)

                    def dn_epi(vector, jd=jd, p=p):
                        tk.wait(vector, "v", t_dmm[p])
                        last = None
                        for s in range(2):
                            last = vector.scalar_tensor_tensor(
                                out=res2_sb[:, jd * TS + s * 512: jd * TS + (s + 1) * 512],
                                in0=pd[p][s][:, :], scalar=sdv[:, jd:jd + 1],
                                in1=x1_bf[:, jd * TS + s * 512: jd * TS + (s + 1) * 512],
                                op0=OP.mult, op1=OP.add)
                        t_res2[p] = tk.tag("vq", last)
                    blk.vector(dn_epi)

                    if jd + 2 < KT:
                        dn_wload(jd + 2)
                        dn_dq(jd + 2)

                t_res2_all = [t_res2[0], t_res2[1]]

            ln2 = contextlib.ExitStack()
            with ln2:
                sq2 = [ln2.enter_context(nc.sbuf_tensor(f"sq2{p}", [128, TS], BF16)) for p in range(2)]
                srow2 = ln2.enter_context(nc.sbuf_tensor("srow2", [1, 3 * TS], F32))
                out_t = [ln2.enter_context(nc.sbuf_tensor(f"outt{p}", [128, TS], F32)) for p in range(2)]
                xtmp2 = ln2.enter_context(nc.sbuf_tensor("xtmp2", [128, 512], F32))
                ps2_s1 = [ln2.enter_context(nc.psum_tensor(f"p2s1{s}", [1, 512], F32)) for s in range(2)]
                ps2_s2 = [ln2.enter_context(nc.psum_tensor(f"p2s2{s}", [1, 512], F32)) for s in range(2)]
                ps2_r = [ln2.enter_context(nc.psum_tensor(f"p2r{s}", [128, 512], F32)) for s in range(2)]
                ps2_q = [ln2.enter_context(nc.psum_tensor(f"p2q{s}", [128, 512], F32)) for s in range(2)]

                t_sq2 = {}
                t_l2te = {}
                t_stat2 = None

                for k in range(KT):
                    p = k % 2

                    def l2_act(scalar, k=k, p=p):
                        tk.wait(scalar, "s", t_res2_all[0], t_res2_all[1],
                                t_l2te.get(k - 2), t_sq2.get(k - 2))
                        if k == 0 and PRELOAD:
                            preload(scalar, 6)
                        tk.tag("aq", scalar.activation(out=sq2[p][:], in_=res2_sb[:, k * TS:(k + 1) * TS],
                                                       func=AF.Square))
                        t_sq2[k] = ("aq", tk.n["aq"])
                    blk.scalar(l2_act)

                    def l2_te(tensor, k=k, p=p):
                        tk.wait(tensor, "t", t_sq2[k])
                        last = None
                        for s in range(2):
                            tensor.matmul(out=ps2_s1[s][:, :], lhsT=onescol[:, 0:1],
                                          rhs=res2_sb[:, k * TS + s * 512: k * TS + (s + 1) * 512],
                                          start=(k == 0), stop=(k == KT - 1))
                            last = tensor.matmul(out=ps2_s2[s][:, :], lhsT=onescol[:, 0:1],
                                                 rhs=sq2[p][:, s * 512:(s + 1) * 512],
                                                 start=(k == 0), stop=(k == KT - 1))
                        t_l2te[k] = tk.tag("tq", last)
                    blk.tensor(l2_te)
                t_stat2 = ("tq", tk.n["tq"])

                def l2_stats_vec(vector):
                    tk.wait(vector, "v", t_stat2)
                    for s in range(2):
                        tk.tag("vq", vector.tensor_scalar(out=srow2[:, s * 512:(s + 1) * 512],
                                                          in0=ps2_s1[s][:, :], scalar1=1.0 / D,
                                                          scalar2=None, op0=OP.mult))
                        tk.tag("vq", vector.tensor_scalar(out=srow2[:, TS + s * 512:TS + (s + 1) * 512],
                                                          in0=ps2_s2[s][:, :], scalar1=1.0 / D,
                                                          scalar2=None, op0=OP.mult))
                blk.vector(l2_stats_vec)
                t_mrow2 = ("vq", tk.n["vq"])

                def l2_stats_act(scalar):
                    tk.wait(scalar, "s", t_mrow2)
                    tk.tag("aq", scalar.activation(out=srow2[:, 2 * TS:3 * TS], in_=srow2[:, 0:TS],
                                                   func=AF.Square))
                blk.scalar(l2_stats_act)
                t_msq2 = ("aq", tk.n["aq"])

                def l2_sv2(vector):
                    tk.wait(vector, "v", t_msq2)
                    tk.tag("vq", vector.tensor_tensor(out=srow2[:, TS:2 * TS], in0=srow2[:, TS:2 * TS],
                                                      in1=srow2[:, 2 * TS:3 * TS], op=OP.subtract))
                blk.vector(l2_sv2)
                t_var2 = ("vq", tk.n["vq"])

                def l2_sa2(scalar):
                    tk.wait(scalar, "s", t_var2)
                    tk.tag("aq", scalar.activation(out=srow2[:, TS:2 * TS], in_=srow2[:, TS:2 * TS],
                                                   func=AF.Ln, bias=epsb[0:1, 0:1]))
                    tk.wait(scalar, "s", ("aq", tk.n["aq"]))
                    tk.tag("aq", scalar.activation(out=srow2[:, TS:2 * TS], in_=srow2[:, TS:2 * TS],
                                                   func=AF.Exp, scale=-0.5))
                blk.scalar(l2_sa2)
                t_rstd2 = ("aq", tk.n["aq"])

                def l2_sv3(vector):
                    tk.wait(vector, "v", t_rstd2)
                    tk.tag("vq", vector.tensor_tensor(out=srow2[:, 0:TS], in0=srow2[:, 0:TS],
                                                      in1=srow2[:, TS:2 * TS], op=OP.mult))
                blk.vector(l2_sv3)
                t_qrow2 = ("vq", tk.n["vq"])

                def l2_bc(tensor):
                    tk.wait(tensor, "t", t_qrow2)
                    last = None
                    for s in range(2):
                        tensor.matmul(out=ps2_r[s][:, :], lhsT=onesrow[0:1, :],
                                      rhs=srow2[0:1, TS + s * 512:TS + (s + 1) * 512],
                                      start=True, stop=True)
                        last = tensor.matmul(out=ps2_q[s][:, :], lhsT=onesrow[0:1, :],
                                             rhs=srow2[0:1, s * 512:(s + 1) * 512],
                                             start=True, stop=True)
                    tk.tag("tq", last)
                blk.tensor(l2_bc)
                t_bc2 = ("tq", tk.n["tq"])

                t_odma = [None, None]
                t_outw = [None, None]
                for k in range(KT):
                    p = k % 2

                    def l2_out(vector, k=k, p=p):
                        tk.wait(vector, "v", t_bc2, t_odma[p])
                        for s in range(2):
                            tk.wait(vector, "v", ("vq", tk.n["vq"]))
                            tk.tag("vq", vector.tensor_tensor(
                                out=xtmp2[:], in0=res2_sb[:, k * TS + s * 512: k * TS + (s + 1) * 512],
                                in1=ps2_r[s][:, :], op=OP.mult))
                            tk.wait(vector, "v", ("vq", tk.n["vq"]))
                            tk.tag("vq", vector.tensor_tensor(
                                out=out_t[p][:, s * 512:(s + 1) * 512], in0=xtmp2[:],
                                in1=ps2_q[s][:, :], op=OP.subtract))
                        t_outw[p] = ("vq", tk.n["vq"])
                    blk.vector(l2_out)

                    def l2_store(sync, k=k, p=p):
                        tk.wait(sync, "y", t_outw[p])
                        t_odma[p] = tk.tag(f"dot{p}", sync.dma_start(
                            out=out_d[k * 128:(k + 1) * 128, :], in_=out_t[p][:]), 16)
                    blk.sync(l2_store)

                if debug:
                    def dbg_l2(sync):
                        tk.wait(sync, "y", t_res2_all[0], t_res2_all[1])
                        tk.tag("ddbg", sync.dma_start(out=dbg_res2[:, :], in_=res2_sb[:]), 16)
                    blk.sync(dbg_l2)

                def fin(sync):
                    tk.wait(sync, "y", t_odma[0], t_odma[1])
                    if debug:
                        tk.wait(sync, "y", ("ddbg", tk.n.get("ddbg", 0)))
                blk.sync(fin)

    return nc


_NC_CACHE = None


def kernel(L0_attn_u, L0_attn_Wdt, L0_attn_bdt, L0_attn_Wb, L0_attn_bb,
           L0_attn_Wc, L0_attn_bc, L0_attn_a_raw, L0_attn_B_base, L0_attn_C_base,
           L0_attn_D, L0_ffn_wq_gate, L0_ffn_s_gate, L0_ffn_wq_up, L0_ffn_s_up,
           L0_ffn_wq_down, L0_ffn_s_down):
    global _NC_CACHE
    u = np.asarray(L0_attn_u, np.float32)[0]
    u_fm = np.ascontiguousarray(u.T)
    Wdt = np.asarray(L0_attn_Wdt, np.float32)
    Wb = np.asarray(L0_attn_Wb, np.float32)
    Wc = np.asarray(L0_attn_Wc, np.float32)
    bdt = np.asarray(L0_attn_bdt, np.float32)
    bb = np.asarray(L0_attn_bb, np.float32)
    bc = np.asarray(L0_attn_bc, np.float32)
    a_raw = np.asarray(L0_attn_a_raw, np.float32)
    B_base = np.asarray(L0_attn_B_base, np.float32)
    C_base = np.asarray(L0_attn_C_base, np.float32)
    Dsk = np.asarray(L0_attn_D, np.float32)
    wqg = np.asarray(L0_ffn_wq_gate, np.int32)
    wqu = np.asarray(L0_ffn_wq_up, np.int32)
    wqd = np.asarray(L0_ffn_wq_down, np.int32)
    sg = np.asarray(L0_ffn_s_gate, np.float32)
    su = np.asarray(L0_ffn_s_up, np.float32)
    sd = np.asarray(L0_ffn_s_down, np.float32)

    wg_t = np.ascontiguousarray(wqg.T.reshape(KT, 128, HT, 128).transpose(2, 1, 0, 3)).reshape(HT * 128, KT * 128)
    wu_t = np.ascontiguousarray(wqu.T.reshape(KT, 128, HT, 128).transpose(2, 1, 0, 3)).reshape(HT * 128, KT * 128)
    wd_t = np.ascontiguousarray(wqd.T.reshape(HT, 128, KT, 128).transpose(2, 1, 0, 3)).reshape(KT * 128, HT * 128)
    sg_a = np.ascontiguousarray(sg.reshape(HT, 128).T)
    su_a = np.ascontiguousarray(su.reshape(HT, 128).T)
    sd_a = np.ascontiguousarray(sd.reshape(KT, 128).T)

    in_maps = []
    for c in range(NC):
        sl = slice(c * DS, (c + 1) * DS)
        bdt_s, bb_s, bc_s = bdt[sl], bb[sl], bc[sl]
        bvec = np.stack([bdt_s[:128], bdt_s[128:], bb_s[:128], bb_s[128:],
                         bc_s[:128], bc_s[128:]], axis=1)
        araw_a = np.ascontiguousarray(a_raw[sl].reshape(2, 128, NST).transpose(1, 0, 2).reshape(128, 2 * NST))
        bbase_a = np.ascontiguousarray(B_base[sl].reshape(2, 128, NST).transpose(1, 0, 2).reshape(128, 2 * NST))
        cbase_a = np.ascontiguousarray(C_base[sl].reshape(2, 128, NST).transpose(1, 0, 2).reshape(128, 2 * NST))
        dvec = np.stack([Dsk[sl][:128], Dsk[sl][128:]], axis=1)
        yidx = np.zeros((128, KT), np.uint32)
        for k in range(KT):
            yidx[:, k] = c * NC * DS + k * 128 + np.arange(128, dtype=np.uint32)
        in_maps.append({
            "u_fm": u_fm,
            "u_ds": np.ascontiguousarray(u_fm[sl, :]),
            "u_tok": np.ascontiguousarray(u_fm[:, c * TS:(c + 1) * TS]),
            "wdt": np.ascontiguousarray(Wdt[:, sl]),
            "wb": np.ascontiguousarray(Wb[:, sl]),
            "wc": np.ascontiguousarray(Wc[:, sl]),
            "bvec": np.ascontiguousarray(bvec),
            "araw": araw_a, "bbase": bbase_a, "cbase": cbase_a,
            "dvec": np.ascontiguousarray(dvec),
            "sgv": sg_a, "suv": su_a, "sdv": sd_a,
            "wg_t": wg_t, "wu_t": wu_t, "wd_t": wd_t,
            "yidx": yidx,
            "ident": np.eye(128, dtype=ml_dtypes.bfloat16),
        })

    if _NC_CACHE is None:
        _NC_CACHE = build_nc()
    res = run_bass_kernel_spmd(_NC_CACHE, in_maps, core_ids=list(range(NC)))
    parts = [np.asarray(res.results[c]["out"], np.float32).T for c in range(NC)]
    out = np.concatenate(parts, axis=0)[None]
    return out


# version 21 (final)
# speedup vs baseline: 1.0866x; 1.0866x over previous
import contextlib
import numpy as np
import ml_dtypes
import concourse.bass as bass
import concourse.mybir as mybir
from concourse.bass_utils import run_bass_kernel_spmd

F32 = mybir.dt.float32
BF16 = mybir.dt.bfloat16
I32 = mybir.dt.int32
U32 = mybir.dt.uint32
AF = mybir.ActivationFunctionType
OP = mybir.AluOpType

D = 2048
L = 8192
NST = 16
H = 5632
NC = 8
DS = D // NC
TS = L // NC
CH = 1024
M = L // CH
KT = D // 128
HT = H // 128
DT_MIN = 1e-4
PRELOAD = False
LN_EPS = 1e-5


class Toks:

    def __init__(self, nc, stack):
        self.nc = nc
        self.stack = stack
        self.h = {}
        self.n = {}
        self.mark = {}

    def sem(self, name):
        if name not in self.h:
            self.h[name] = self.stack.enter_context(self.nc.semaphore(name))
            self.n[name] = 0
        return self.h[name]

    def tag(self, name, inst, amt=1):
        s = self.sem(name)
        inst.then_inc(s, amt)
        self.n[name] += amt
        return (name, self.n[name])

    def wait(self, eng, ename, *tokens):
        for t in tokens:
            if t is None:
                continue
            name, v = t
            if v <= 0:
                continue
            key = (ename, name)
            if self.mark.get(key, 0) >= v:
                continue
            eng.wait_ge(self.sem(name), v)
            self.mark[key] = v


def build_nc(debug=False):
    nc = bass.Bass("TRN2", target_bir_lowering=False)

    u_fm = nc.dram_tensor("u_fm", [D, L], F32, kind="ExternalInput")
    u_ds = nc.dram_tensor("u_ds", [DS, L], F32, kind="ExternalInput")
    u_tok = nc.dram_tensor("u_tok", [D, TS], F32, kind="ExternalInput")
    wdt = nc.dram_tensor("wdt", [D, DS], F32, kind="ExternalInput")
    wb = nc.dram_tensor("wb", [D, DS], F32, kind="ExternalInput")
    wc = nc.dram_tensor("wc", [D, DS], F32, kind="ExternalInput")
    bvec_d = nc.dram_tensor("bvec", [128, 6], F32, kind="ExternalInput")
    araw_d = nc.dram_tensor("araw", [128, 2 * NST], F32, kind="ExternalInput")
    bbase_d = nc.dram_tensor("bbase", [128, 2 * NST], F32, kind="ExternalInput")
    cbase_d = nc.dram_tensor("cbase", [128, 2 * NST], F32, kind="ExternalInput")
    dvec_d = nc.dram_tensor("dvec", [128, 2], F32, kind="ExternalInput")
    sg_d = nc.dram_tensor("sgv", [128, HT], F32, kind="ExternalInput")
    su_d = nc.dram_tensor("suv", [128, HT], F32, kind="ExternalInput")
    sd_d = nc.dram_tensor("sdv", [128, KT], F32, kind="ExternalInput")
    wg_d = nc.dram_tensor("wg_t", [HT * 128, KT * 128], I32, kind="ExternalInput")
    wu_d = nc.dram_tensor("wu_t", [HT * 128, KT * 128], I32, kind="ExternalInput")
    wd_d = nc.dram_tensor("wd_t", [KT * 128, HT * 128], I32, kind="ExternalInput")
    yidx_d = nc.dram_tensor("yidx", [128, KT], U32, kind="ExternalInput")
    ident_d = nc.dram_tensor("ident", [128, 128], BF16, kind="ExternalInput")
    out_d = nc.dram_tensor("out", [D, TS], F32, kind="ExternalOutput")

    if debug:
        dbg_ubf = nc.dram_tensor("dbg_ubf", [128, KT * CH], BF16, kind="ExternalOutput")
        dbg_dt = nc.dram_tensor("dbg_dt", [128, 2 * CH], BF16, kind="ExternalOutput")
        dbg_g = nc.dram_tensor("dbg_g", [128, 2 * CH], BF16, kind="ExternalOutput")
        dbg_y = nc.dram_tensor("dbg_y", [DS, L], BF16, kind="ExternalOutput")
        dbg_res = nc.dram_tensor("dbg_res", [128, KT * TS], F32, kind="ExternalOutput")
        dbg_x1 = nc.dram_tensor("dbg_x1", [128, KT * TS], BF16, kind="ExternalOutput")
        dbg_res2 = nc.dram_tensor("dbg_res2", [128, KT * TS], BF16, kind="ExternalOutput")
        dbg_sgv = nc.dram_tensor("dbg_sgv", [128, HT], F32, kind="ExternalOutput")
        dbg_sg = nc.dram_tensor("dbg_sg", [128, TS], F32, kind="ExternalOutput")
        dbg_hact = nc.dram_tensor("dbg_hact", [128, TS], BF16, kind="ExternalOutput")
        dbg_wg = nc.dram_tensor("dbg_wg", [128, KT * 128], BF16, kind="ExternalOutput")
    y_ag_in = nc.dram_tensor("y_ag_in", [M * DS, CH], BF16)
    y_all8 = nc.dram_tensor("y_all8", [M * NC * DS, CH], BF16, addr_space="Shared")

    st = contextlib.ExitStack()
    with st:
        tk = Toks(nc, st)
        blk = st.enter_context(nc.Block())

        bvec = st.enter_context(nc.sbuf_tensor("bvec_t", [128, 6], F32))
        at = st.enter_context(nc.sbuf_tensor("at", [128, 2 * NST], F32))
        cb = st.enter_context(nc.sbuf_tensor("cb", [128, 2 * NST], F32))
        dvec = st.enter_context(nc.sbuf_tensor("dvec_t", [128, 2], F32))
        sgv = st.enter_context(nc.sbuf_tensor("sgv_t", [128, HT], F32))
        suv = st.enter_context(nc.sbuf_tensor("suv_t", [128, HT], F32))
        sdv = st.enter_context(nc.sbuf_tensor("sdv_t", [128, KT], F32))
        yidx = st.enter_context(nc.sbuf_tensor("yidx_t", [128, KT], U32))
        onescol = st.enter_context(nc.sbuf_tensor("onescol", [128, 1], BF16))
        onesrow = st.enter_context(nc.sbuf_tensor("onesrow", [1, 128], F32))
        epsb = st.enter_context(nc.sbuf_tensor("epsb", [1, 1], F32))
        ident = st.enter_context(nc.sbuf_tensor("ident_t", [128, 128], BF16))

        def preload(scalar, set_id):
            return scalar.add_instruction(mybir.InstLoadActFuncSet(
                name=scalar.bass.get_next_instruction_name(),
                act_func_set_id=set_id, ins=[], outs=[]))

        sb = contextlib.ExitStack()
        with sb:
            wdt_bf = sb.enter_context(nc.sbuf_tensor("wdt_bf", [128, KT * DS], BF16))
            wb_bf = sb.enter_context(nc.sbuf_tensor("wb_bf", [128, KT * DS], BF16))
            wc_bf = sb.enter_context(nc.sbuf_tensor("wc_bf", [128, KT * DS], BF16))
            araw = sb.enter_context(nc.sbuf_tensor("araw_t", [128, 2 * NST], F32))
            bbase = sb.enter_context(nc.sbuf_tensor("bbase_t", [128, 2 * NST], F32))
            cbase = sb.enter_context(nc.sbuf_tensor("cbase_t", [128, 2 * NST], F32))
            ubf = [sb.enter_context(nc.sbuf_tensor(f"ubf{p}", [128, KT * CH], BF16)) for p in range(2)]
            usl = [sb.enter_context(nc.sbuf_tensor(f"usl{p}", [128, 2 * CH], BF16)) for p in range(2)]
            dt_sb = [sb.enter_context(nc.sbuf_tensor(f"dt{p}", [128, 2 * CH], BF16)) for p in range(2)]
            b_sb = sb.enter_context(nc.sbuf_tensor("b_sb", [128, 2 * CH], BF16))
            c_sb = [sb.enter_context(nc.sbuf_tensor(f"c{p}", [128, 2 * CH], BF16)) for p in range(2)]
            g_sb = [sb.enter_context(nc.sbuf_tensor(f"g{p}", [128, 2 * CH], BF16)) for p in range(2)]
            gtmp = sb.enter_context(nc.sbuf_tensor("gtmp", [128, CH], F32))
            abar = [sb.enter_context(nc.sbuf_tensor(f"abar{p}", [128, CH], BF16)) for p in range(8)]
            xbuf = [sb.enter_context(nc.sbuf_tensor(f"xbuf{p}", [128, CH], BF16)) for p in range(4)]
            cbd = sb.enter_context(nc.sbuf_tensor("cbd", [128, 2 * NST * 128], BF16))
            ytmp = sb.enter_context(nc.sbuf_tensor("ytmp", [128, CH], F32))
            y_out = [sb.enter_context(nc.sbuf_tensor(f"yout{p}", [128, 2 * CH], BF16)) for p in range(2)]
            stcols = sb.enter_context(nc.sbuf_tensor("stcols", [128, 2 * NST], BF16))
            sp_a = [sb.enter_context(nc.sbuf_tensor(f"spa{p}", [128, 512], F32)) for p in range(2)]
            sp_b = [sb.enter_context(nc.sbuf_tensor(f"spb{p}", [128, 512], F32)) for p in range(2)]
            rt = [sb.enter_context(nc.sbuf_tensor(f"rt{p}", [128, 512], F32)) for p in range(2)]
            pdt = [sb.enter_context(nc.psum_tensor(f"pdt{p}", [128, 512], F32)) for p in range(2)]
            pb = [sb.enter_context(nc.psum_tensor(f"pb{p}", [128, 512], F32)) for p in range(2)]
            pc = [sb.enter_context(nc.psum_tensor(f"pc{p}", [128, 512], F32)) for p in range(2)]
            pacc = [sb.enter_context(nc.psum_tensor(f"pacc{s}", [128, 512], F32)) for s in range(2)]

            def a_sync(sync):
                for dst, src in [(bvec, bvec_d), (araw, araw_d), (bbase, bbase_d),
                                 (cbase, cbase_d), (dvec, dvec_d), (sgv, sg_d),
                                 (suv, su_d), (sdv, sd_d), (yidx, yidx_d),
                                 (ident, ident_d)]:
                    tk.tag("dcon", sync.dma_start(out=dst[:], in_=src[:, :]), 16)
            blk.sync(a_sync)
            t_consts = ("dcon", tk.n["dcon"])

            def a_gp(gp):
                for w_bf, w_d in [(wdt_bf, wdt), (wb_bf, wb), (wc_bf, wc)]:
                    tk.tag("dwp", gp.dma_start(
                        out=w_bf[:].rearrange("p (k f) -> p k f", k=KT),
                        in_=w_d[:, :].rearrange("(k p) f -> p k f", k=KT)), 16)
            blk.gpsimd(a_gp)
            t_wload = ("dwp", tk.n["dwp"])

            def a_vec(vector):
                tk.wait(vector, "v", t_consts)
                t = tk.tag("vq", vector.tensor_scalar(out=sp_a[0][:, 0:2 * NST], in0=araw[:],
                                                      scalar1=0.0, scalar2=-1.0, op0=OP.max, op1=OP.mult))
                tk.tag("vq", vector.tensor_tensor(out=cb[:], in0=bbase[:], in1=cbase[:], op=OP.mult))
                tk.tag("vq", vector.memset(onescol[:], 1.0))
                tk.tag("vq", vector.memset(onesrow[:], 1.0))
                tk.tag("vq", vector.memset(epsb[:], LN_EPS))
                return t
            blk.vector(a_vec)
            t_rneg = ("vq", tk.n["vq"] - 3)

            def a_act(scalar):
                tk.wait(scalar, "s", t_consts)
                if PRELOAD:
                    preload(scalar, 6)
                tk.tag("aq", scalar.activation(out=sp_b[0][:, 0:2 * NST], in_=araw[:], func=AF.Abs))
                tk.wait(scalar, "s", ("aq", tk.n["aq"]))
                tk.tag("aq", scalar.activation(out=sp_b[0][:, 0:2 * NST], in_=sp_b[0][:, 0:2 * NST],
                                               func=AF.Exp, scale=-1.0))
                tk.wait(scalar, "s", ("aq", tk.n["aq"]))
                tk.tag("aq", scalar.activation(out=sp_b[0][:, 0:2 * NST], in_=sp_b[0][:, 0:2 * NST],
                                               func=AF.Ln, bias=1.0))
            blk.scalar(a_act)
            t_ln1p = ("aq", tk.n["aq"])

            def a_vec2(vector):
                tk.wait(vector, "v", t_ln1p, t_rneg)
                tk.tag("vq", vector.scalar_tensor_tensor(out=at[:], in0=sp_b[0][:, 0:2 * NST],
                                                         scalar=-1.0, in1=sp_a[0][:, 0:2 * NST],
                                                         op0=OP.mult, op1=OP.add))
                for col in range(2 * NST):
                    tk.tag("vq", vector.tensor_scalar(
                        out=cbd[:, col * 128:(col + 1) * 128], in0=ident[:],
                        scalar1=cb[:, col:col + 1], scalar2=None, op0=OP.mult))
            blk.vector(a_vec2)
            t_at = ("vq", tk.n["vq"])
            t_cb = t_at

            t_ucast = [None, None]
            t_uslc = [None, None]
            t_proj_done = [None, None]
            t_epi_psum = [None] * 4
            t_epi_act = [None] * 4
            t_dt = {}
            t_bb = {}
            t_cc = {}
            t_g = {}
            t_scan = {}
            t_red = {}
            t_stcol = {}
            t_abar = {}
            t_ydma = [None, None]
            t_ystt = [None, None]
            t_c_used = [None, None]
            t_dt_used = [None, None]
            t_gu = {}
            t_usl_used = [None, None]

            t_yacc = [None]
            t_epi_mm_all = {}

            def emit_ag(m):
                par = m % 2

                def b_ag(gp, m=m, par=par):
                    tk.wait(gp, "g", t_ydma[par])
                    tk.tag("cs", gp.collective_compute(
                        "AllGather", OP.bypass, replica_groups=[list(range(NC))],
                        ins=[y_ag_in[m * DS:(m + 1) * DS, :].opt()],
                        outs=[y_all8[m * NC * DS:(m + 1) * NC * DS, :].opt()]))
                blk.gpsimd(b_ag)

            def emit_bgp(m):
                par = m % 2

                def b_gp(gp, m=m, par=par):
                    tk.wait(gp, "g", t_proj_done[par])
                    t_ucast[par] = tk.tag(f"du{par}", gp.dma_start(
                        out=ubf[par][:].rearrange("p (k t) -> p k t", k=KT),
                        in_=u_fm[:, m * CH:(m + 1) * CH].rearrange("(k p) t -> p k t", k=KT)), 16)
                    tk.wait(gp, "g", t_usl_used[par])
                    t_uslc[par] = tk.tag(f"dsl{par}", gp.dma_start(
                        out=usl[par][:].rearrange("p (j t) -> p j t", j=2),
                        in_=u_ds[:, m * CH:(m + 1) * CH].rearrange("(j p) t -> p j t", j=2)), 16)
                blk.gpsimd(b_gp)
                if m >= 2:
                    emit_ag(m - 2)

            def emit_proj_js(m, js):
                par = m % 2
                j, s = js >> 1, js & 1
                p2 = js % 2

                def te_body(tensor, m=m, par=par, js=js, j=j, s=s, p2=p2):
                    tk.wait(tensor, "t", t_ucast[par], t_wload,
                            t_epi_psum[p2], t_epi_act[p2])
                    last = None
                    for k in range(KT):
                        rhs = ubf[par][:, k * CH + s * 512: k * CH + (s + 1) * 512]
                        tensor.matmul(out=pdt[p2][:, :],
                                      lhsT=wdt_bf[:, k * DS + j * 128: k * DS + (j + 1) * 128],
                                      rhs=rhs, start=(k == 0), stop=(k == KT - 1))
                        tensor.matmul(out=pb[p2][:, :],
                                      lhsT=wb_bf[:, k * DS + j * 128: k * DS + (j + 1) * 128],
                                      rhs=rhs, start=(k == 0), stop=(k == KT - 1))
                        last = tensor.matmul(out=pc[p2][:, :],
                                             lhsT=wc_bf[:, k * DS + j * 128: k * DS + (j + 1) * 128],
                                             rhs=rhs, start=(k == 0), stop=(k == KT - 1))
                    t_epi_mm_all[(m, js)] = tk.tag("tq", last)
                blk.tensor(te_body)

                def act_epi(scalar, m=m, par=par, js=js, j=j, s=s, p2=p2):
                    tk.wait(scalar, "s", t_epi_mm_all[(m, js)], t_at, t_epi_act[p2])
                    tk.tag("aq", scalar.activation(out=sp_a[p2][:], in_=pdt[p2][:, :],
                                                   func=AF.Abs, bias=bvec[:, j:j + 1]))
                    tk.wait(scalar, "s", ("aq", tk.n["aq"]))
                    tk.tag("aq", scalar.activation(out=sp_b[p2][:], in_=sp_a[p2][:],
                                                   func=AF.Exp, scale=-1.0))
                    tk.wait(scalar, "s", ("aq", tk.n["aq"]))
                    t_epi_act[p2] = tk.tag("aq", scalar.activation(out=sp_b[p2][:], in_=sp_b[p2][:],
                                                                   func=AF.Ln, bias=1.0))
                blk.scalar(act_epi)
                t_lnjs = ("aq", tk.n["aq"])

                def vec_epi(vector, m=m, par=par, js=js, j=j, s=s, p2=p2, t_lnjs=t_lnjs):
                    tk.wait(vector, "v", t_epi_mm_all[(m, js)], t_epi_psum[p2],
                            t_g.get((m - 1, j)))
                    tk.tag("vq", vector.tensor_scalar(out=rt[p2][:], in0=pdt[p2][:, :],
                                                      scalar1=bvec[:, j:j + 1], scalar2=0.0,
                                                      op0=OP.add, op1=OP.max))
                    tk.wait(vector, "v", t_dt_used[par], t_lnjs, ("vq", tk.n["vq"]))
                    tk.tag("vq", vector.scalar_tensor_tensor(
                        out=dt_sb[par][:, j * CH + s * 512: j * CH + (s + 1) * 512],
                        in0=rt[p2][:], scalar=DT_MIN, in1=sp_b[p2][:], op0=OP.add, op1=OP.add))
                    t_dt[(m, j, s)] = ("vq", tk.n["vq"])
                    tk.tag("vq", vector.tensor_scalar(
                        out=b_sb[:, j * CH + s * 512: j * CH + (s + 1) * 512],
                        in0=pb[p2][:, :], scalar1=bvec[:, 2 + j:3 + j], scalar2=None, op0=OP.add))
                    t_bb[(m, j, s)] = ("vq", tk.n["vq"])
                    tk.wait(vector, "v", t_c_used[par])
                    t_epi_psum[p2] = tk.tag("vq", vector.tensor_scalar(
                        out=c_sb[par][:, j * CH + s * 512: j * CH + (s + 1) * 512],
                        in0=pc[p2][:, :], scalar1=bvec[:, 4 + j:5 + j], scalar2=None, op0=OP.add))
                    t_cc[(m, j, s)] = t_epi_psum[p2]
                blk.vector(vec_epi)

                if s == 1:
                    def vec_g(vector, m=m, par=par, j=j):
                        tk.wait(vector, "v", t_dt[(m, j, 0)], t_dt[(m, j, 1)],
                                t_bb[(m, j, 0)], t_bb[(m, j, 1)], t_uslc[par])
                        tk.tag("vq", vector.tensor_tensor(
                            out=gtmp[:], in0=dt_sb[par][:, j * CH:(j + 1) * CH],
                            in1=b_sb[:, j * CH:(j + 1) * CH], op=OP.mult))
                        tk.wait(vector, "v", t_gu.get((par, j)), ("vq", tk.n["vq"]))
                        tk.tag("vq", vector.tensor_tensor(
                            out=g_sb[par][:, j * CH:(j + 1) * CH], in0=gtmp[:],
                            in1=usl[par][:, j * CH:(j + 1) * CH], op=OP.mult))
                        t_g[(m, j)] = ("vq", tk.n["vq"])
                    blk.vector(vec_g)
                if js == 3:
                    t_proj_done[par] = t_epi_mm_all[(m, 3)]

            def prev_nk(m, j, n):
                if n >= 2:
                    return (m, j, n - 2)
                if m >= 1:
                    return (m - 1, j, NST - 2 + n)
                return None

            def prev_ab(m, j, n):
                if n >= 4:
                    return (m, j, n - 4)
                if m >= 1:
                    return (m - 1, j, NST - 4 + n)
                return None

            def emit_scan_n(m, j, n):
                par = m % 2
                q = j * 2 + (n & 1)
                pk = prev_nk(m, j, n)

                if n % 4 == 0:
                    def act_abar4(scalar, m=m, par=par, j=j, n0=n):
                        tk.wait(scalar, "s", t_dt[(m, j, 0)], t_dt[(m, j, 1)])
                        for nn in range(n0, n0 + 4):
                            qa = j * 4 + nn % 4
                            pka = prev_ab(m, j, nn)
                            tk.wait(scalar, "s", t_scan.get(pka), t_abar.get(pka))
                            t_abar[(m, j, nn)] = tk.tag("aq", scalar.activation(
                                out=abar[qa][:], in_=dt_sb[par][:, j * CH:(j + 1) * CH],
                                func=AF.Exp, scale=at[:, j * NST + nn:j * NST + nn + 1]))
                    blk.scalar(act_abar4)

                def vec_scan_n(vector, m=m, par=par, j=j, n=n, q=q, pk=pk):
                    tk.wait(vector, "v", t_abar[(m, j, n)], t_g[(m, j)],
                            t_red.get(pk), t_at,
                            t_stcol.get((m - 1, j, n)))
                    init = 0.0 if m == 0 else stcols[:, j * NST + n:j * NST + n + 1]
                    tk.tag("vq", vector.tensor_tensor_scan(
                        out=xbuf[q][:], data0=abar[j * 4 + n % 4][:],
                        data1=g_sb[par][:, j * CH:(j + 1) * CH],
                        initial=init, op0=OP.mult, op1=OP.add))
                    t_scan[(m, j, n)] = ("vq", tk.n["vq"])
                    tk.wait(vector, "v", t_scan[(m, j, n)])
                    tk.tag("vq", vector.tensor_copy(
                        out=stcols[:, j * NST + n:j * NST + n + 1],
                        in_=xbuf[q][:, CH - 1:CH]))
                    t_stcol[(m, j, n)] = ("vq", tk.n["vq"])
                blk.vector(vec_scan_n)

                def te_red_n(tensor, m=m, par=par, j=j, n=n, q=q):
                    tk.wait(tensor, "t", t_scan[(m, j, n)], t_at)
                    if n == 0:
                        tk.wait(tensor, "t", t_yacc[0])
                    last = None
                    for s in range(2):
                        last = tensor.matmul(
                            out=pacc[s][:, :],
                            lhsT=cbd[:, (j * NST + n) * 128:(j * NST + n + 1) * 128],
                            rhs=xbuf[q][:, s * 512:(s + 1) * 512],
                            start=(n == 0), stop=(n == NST - 1))
                    t_red[(m, j, n)] = tk.tag("tq", last)
                blk.tensor(te_red_n)

                if n == NST - 1:
                    t_gu[(par, j)] = t_scan[(m, j, n)]
                    if j == 1:
                        t_dt_used[par] = t_abar[(m, j, n)]

            def emit_y(m, j):
                par = m % 2

                def vec_y(vector, m=m, par=par, j=j):
                    tk.wait(vector, "v", t_red[(m, j, NST - 1)],
                            t_cc[(m, j, 0)], t_cc[(m, j, 1)], t_uslc[par], t_ydma[par])
                    for s in range(2):
                        tk.tag("vq", vector.tensor_tensor(
                            out=ytmp[:, s * 512:(s + 1) * 512], in0=pacc[s][:, :],
                            in1=c_sb[par][:, j * CH + s * 512: j * CH + (s + 1) * 512],
                            op=OP.mult))
                    t_yacc[0] = ("vq", tk.n["vq"])
                    tk.wait(vector, "v", t_yacc[0])
                    for s in range(2):
                        tk.tag("vq", vector.scalar_tensor_tensor(
                            out=y_out[par][:, j * CH + s * 512: j * CH + (s + 1) * 512],
                            in0=usl[par][:, j * CH + s * 512: j * CH + (s + 1) * 512],
                            scalar=dvec[:, j:j + 1],
                            in1=ytmp[:, s * 512:(s + 1) * 512], op0=OP.mult, op1=OP.add))
                    t_ystt[par] = ("vq", tk.n["vq"])
                    if j == 1:
                        t_usl_used[par] = t_ystt[par]
                        t_c_used[par] = t_ystt[par]
                blk.vector(vec_y)

            def emit_ysync(m):
                par = m % 2

                def b_sync(sync, m=m, par=par):
                    tk.wait(sync, "y", t_ystt[par])
                    t_ydma[par] = tk.tag(f"dy{par}", sync.dma_start(
                        out=y_ag_in[m * DS:(m + 1) * DS, :].rearrange("(j p) t -> p j t", j=2),
                        in_=y_out[par][:].rearrange("p (j t) -> p j t", j=2)), 16)
                    if debug:
                        t_ydma[par] = tk.tag(f"dy{par}", sync.dma_start(
                            out=dbg_y[:, m * CH:(m + 1) * CH].rearrange("(j p) t -> p j t", j=2),
                            in_=y_out[par][:].rearrange("p (j t) -> p j t", j=2)), 16)
                blk.sync(b_sync)

            emit_bgp(0)
            for js in range(4):
                emit_proj_js(0, js)
            for m in range(1, M):
                emit_bgp(m)
                emit_proj_js(m, 0)
                emit_proj_js(m, 1)
                for n in range(NST):
                    emit_scan_n(m - 1, 0, n)
                emit_y(m - 1, 0)
                emit_proj_js(m, 2)
                emit_proj_js(m, 3)
                for n in range(NST):
                    emit_scan_n(m - 1, 1, n)
                emit_y(m - 1, 1)
                emit_ysync(m - 1)
            for n in range(NST):
                emit_scan_n(M - 1, 0, n)
            emit_y(M - 1, 0)
            for n in range(NST):
                emit_scan_n(M - 1, 1, n)
            emit_y(M - 1, 1)
            emit_ysync(M - 1)
            emit_ag(M - 2)
            emit_ag(M - 1)

            t_ylast0 = ("dy0", tk.n["dy0"])
            t_ylast1 = ("dy1", tk.n["dy1"])

        t_ag = ("cs", tk.n["cs"])
        t_b_done = [t_ylast0, t_ylast1, ("tq", tk.n["tq"]), ("aq", tk.n["aq"]),
                    ("vq", tk.n["vq"])]

        sdst = contextlib.ExitStack()
        with sdst:
            x1_bf = sdst.enter_context(nc.sbuf_tensor("x1bf", [128, KT * TS], BF16))
            wg_raw = [sdst.enter_context(nc.sbuf_tensor(f"wgr{p}", [128, KT * 128], BF16)) for p in range(3)]
            wu_raw = [sdst.enter_context(nc.sbuf_tensor(f"wur{p}", [128, KT * 128], BF16)) for p in range(3)]

            t_wcast = [None, None, None]
            t_wdq3 = [None, None, None]
            t_gmm3 = [None, None, None]
            t_wdc = [None, None]
            t_wddq2 = [None, None]
            t_dte = [None, None]

            def f_wload(i):
                p3 = i % 3

                def body(gp, i=i, p3=p3):
                    tk.wait(gp, "g", t_gmm3[p3], t_wdq3[p3])
                    if i < 3:
                        tk.wait(gp, "g", *t_b_done)
                    tk.tag(f"dwc{p3}", gp.dma_start(
                        out=wg_raw[p3][:],
                        in_=wg_d[i * 128:(i + 1) * 128, :]), 16)
                    t_wcast[p3] = tk.tag(f"dwc{p3}", gp.dma_start(
                        out=wu_raw[p3][:],
                        in_=wu_d[i * 128:(i + 1) * 128, :]), 16)
                blk.gpsimd(body)


            for i in range(3):
                f_wload(i)

            ln1 = contextlib.ExitStack()
            with ln1:
                res_sb = ln1.enter_context(nc.sbuf_tensor("res_sb", [128, KT * TS], F32))
                ut = [ln1.enter_context(nc.sbuf_tensor(f"ut{p}", [128, TS], F32)) for p in range(2)]
                ybf = [ln1.enter_context(nc.sbuf_tensor(f"ybf{p}", [128, TS], BF16)) for p in range(2)]
                rbf = [ln1.enter_context(nc.sbuf_tensor(f"rbf{p}", [128, TS], BF16)) for p in range(2)]
                sqbf = [ln1.enter_context(nc.sbuf_tensor(f"sqbf{p}", [128, TS], BF16)) for p in range(2)]
                srow = ln1.enter_context(nc.sbuf_tensor("srow", [1, 3 * TS], F32))
                ps_s1 = [ln1.enter_context(nc.psum_tensor(f"ps1{s}", [1, 512], F32)) for s in range(2)]
                ps_s2 = [ln1.enter_context(nc.psum_tensor(f"ps2{s}", [1, 512], F32)) for s in range(2)]
                ps_r = [ln1.enter_context(nc.psum_tensor(f"psr{s}", [128, 512], F32)) for s in range(2)]
                ps_q = [ln1.enter_context(nc.psum_tensor(f"psq{s}", [128, 512], F32)) for s in range(2)]
                xtmp = ln1.enter_context(nc.sbuf_tensor("xtmp", [128, 512], F32))

                t_ut = [None, None]
                t_yb = [None, None]
                t_res = {}
                t_rbf = {}
                t_sq = {}
                t_resk_used = [None, None]
                t_stat_mm = None

                for k in range(KT):
                    p = k % 2

                    def d_sync(sync, k=k, p=p):
                        tk.wait(sync, "y", t_resk_used[p])
                        t_ut[p] = tk.tag(f"dut{p}", sync.dma_start(
                            out=ut[p][:], in_=u_tok[k * 128:(k + 1) * 128, :]), 16)
                    blk.sync(d_sync)

                    def d_gp(gp, k=k, p=p):
                        tk.wait(gp, "g", t_ag, t_resk_used[p], t_consts)
                        t_yb[p] = tk.tag(f"dyb{p}", gp.indirect_dma_start(
                            out=ybf[p][:], out_offset=None,
                            in_=y_all8.ap(),
                            in_offset=bass.IndirectOffsetOnAxis(ap=yidx[:, k:k + 1], axis=0)), 16)
                    blk.gpsimd(d_gp)

                    def d_vec(vector, k=k, p=p):
                        tk.wait(vector, "v", t_ut[p], t_yb[p], t_resk_used[p])
                        tk.tag("vq", vector.tensor_tensor(
                            out=res_sb[:, k * TS:(k + 1) * TS], in0=ut[p][:], in1=ybf[p][:], op=OP.add))
                        t_res[k] = ("vq", tk.n["vq"])
                        tk.wait(vector, "v", t_res[k])
                        tk.tag("vq", vector.tensor_copy(out=rbf[p][:], in_=res_sb[:, k * TS:(k + 1) * TS]))
                        t_rbf[k] = ("vq", tk.n["vq"])
                    blk.vector(d_vec)

                    def d_act(scalar, k=k, p=p):
                        tk.wait(scalar, "s", t_res[k], t_resk_used[p], t_sq.get(k - 2))
                        tk.tag("aq", scalar.activation(out=sqbf[p][:], in_=res_sb[:, k * TS:(k + 1) * TS],
                                                       func=AF.Square))
                        t_sq[k] = ("aq", tk.n["aq"])
                    blk.scalar(d_act)

                    def d_te(tensor, k=k, p=p):
                        tk.wait(tensor, "t", t_rbf[k], t_sq[k])
                        last = None
                        for s in range(2):
                            tensor.matmul(out=ps_s1[s][:, :], lhsT=onescol[:, 0:1],
                                          rhs=rbf[p][:, s * 512:(s + 1) * 512],
                                          start=(k == 0), stop=(k == KT - 1))
                            last = tensor.matmul(out=ps_s2[s][:, :], lhsT=onescol[:, 0:1],
                                                 rhs=sqbf[p][:, s * 512:(s + 1) * 512],
                                                 start=(k == 0), stop=(k == KT - 1))
                        t_resk_used[p] = tk.tag("tq", last)
                    blk.tensor(d_te)
                t_stat_mm = t_resk_used[(KT - 1) % 2]

                def d_stats_vec(vector):
                    tk.wait(vector, "v", t_stat_mm)
                    for s in range(2):
                        tk.tag("vq", vector.tensor_scalar(out=srow[:, s * 512:(s + 1) * 512],
                                                          in0=ps_s1[s][:, :], scalar1=1.0 / D,
                                                          scalar2=None, op0=OP.mult))
                        tk.tag("vq", vector.tensor_scalar(out=srow[:, TS + s * 512:TS + (s + 1) * 512],
                                                          in0=ps_s2[s][:, :], scalar1=1.0 / D,
                                                          scalar2=None, op0=OP.mult))
                blk.vector(d_stats_vec)
                t_mrow = ("vq", tk.n["vq"])

                def d_stats_act(scalar):
                    tk.wait(scalar, "s", t_mrow)
                    tk.tag("aq", scalar.activation(out=srow[:, 2 * TS:3 * TS], in_=srow[:, 0:TS],
                                                   func=AF.Square))
                blk.scalar(d_stats_act)
                t_msq = ("aq", tk.n["aq"])

                def d_stats_vec2(vector):
                    tk.wait(vector, "v", t_msq)
                    tk.tag("vq", vector.tensor_tensor(out=srow[:, TS:2 * TS], in0=srow[:, TS:2 * TS],
                                                      in1=srow[:, 2 * TS:3 * TS], op=OP.subtract))
                blk.vector(d_stats_vec2)
                t_var = ("vq", tk.n["vq"])

                def d_stats_act2(scalar):
                    tk.wait(scalar, "s", t_var)
                    tk.tag("aq", scalar.activation(out=srow[:, TS:2 * TS], in_=srow[:, TS:2 * TS],
                                                   func=AF.Ln, bias=epsb[0:1, 0:1]))
                    tk.wait(scalar, "s", ("aq", tk.n["aq"]))
                    tk.tag("aq", scalar.activation(out=srow[:, TS:2 * TS], in_=srow[:, TS:2 * TS],
                                                   func=AF.Exp, scale=-0.5))
                blk.scalar(d_stats_act2)
                t_rstd = ("aq", tk.n["aq"])

                def d_stats_vec3(vector):
                    tk.wait(vector, "v", t_rstd)
                    tk.tag("vq", vector.tensor_tensor(out=srow[:, 0:TS], in0=srow[:, 0:TS],
                                                      in1=srow[:, TS:2 * TS], op=OP.mult))
                blk.vector(d_stats_vec3)
                t_qrow = ("vq", tk.n["vq"])

                def d_bcast(tensor):
                    tk.wait(tensor, "t", t_qrow)
                    last = None
                    for s in range(2):
                        tensor.matmul(out=ps_r[s][:, :], lhsT=onesrow[0:1, :],
                                      rhs=srow[0:1, TS + s * 512:TS + (s + 1) * 512],
                                      start=True, stop=True)
                        last = tensor.matmul(out=ps_q[s][:, :], lhsT=onesrow[0:1, :],
                                             rhs=srow[0:1, s * 512:(s + 1) * 512],
                                             start=True, stop=True)
                    tk.tag("tq", last)
                blk.tensor(d_bcast)
                t_bc = ("tq", tk.n["tq"])

                def d_x1(vector):
                    tk.wait(vector, "v", t_bc)
                    for k in range(KT):
                        for s in range(2):
                            tk.wait(vector, "v", ("vq", tk.n["vq"]))
                            tk.tag("vq", vector.tensor_tensor(
                                out=xtmp[:], in0=res_sb[:, k * TS + s * 512: k * TS + (s + 1) * 512],
                                in1=ps_r[s][:, :], op=OP.mult))
                            tk.wait(vector, "v", ("vq", tk.n["vq"]))
                            tk.tag("vq", vector.tensor_tensor(
                                out=x1_bf[:, k * TS + s * 512: k * TS + (s + 1) * 512],
                                in0=xtmp[:], in1=ps_q[s][:, :], op=OP.subtract))
                blk.vector(d_x1)
                t_x1 = ("vq", tk.n["vq"])

                if debug:
                    def dbg_ln1(sync):
                        tk.wait(sync, "y", t_x1)
                        tk.tag("ddbg", sync.dma_start(out=dbg_res[:, :], in_=res_sb[:]), 16)
                        tk.tag("ddbg", sync.dma_start(out=dbg_x1[:, :], in_=x1_bf[:]), 16)
                    blk.sync(dbg_ln1)

            ffn = contextlib.ExitStack()
            with ffn:
                hact = sdst.enter_context(nc.sbuf_tensor("hact", [128, HT * TS], BF16))
                sg_act = [ffn.enter_context(nc.sbuf_tensor(f"sga{p}", [128, TS], F32)) for p in range(2)]
                pg = [[ffn.enter_context(nc.psum_tensor(f"pg{p}{s}", [128, 512], F32)) for s in range(2)] for p in range(2)]
                pu = [[ffn.enter_context(nc.psum_tensor(f"pu{p}{s}", [128, 512], F32)) for s in range(2)] for p in range(2)]

                t_gmm = [None, None]
                t_hact = [None, None]
                t_silu = [None, None]

                def f_dq(i):
                    p3 = i % 3

                    def body(gp, i=i, p3=p3):
                        tk.wait(gp, "g", t_wcast[p3], t_gmm3[p3])
                        tk.tag("gq", gp.tensor_scalar(out=wg_raw[p3][:], in0=wg_raw[p3][:],
                                                      scalar1=-128.0, scalar2=1.0,
                                                      op0=OP.add, op1=OP.mult))
                        tk.tag("gq", gp.tensor_scalar(out=wu_raw[p3][:], in0=wu_raw[p3][:],
                                                      scalar1=-128.0, scalar2=1.0,
                                                      op0=OP.add, op1=OP.mult))
                        t_wdq3[p3] = ("gq", tk.n["gq"])
                    blk.gpsimd(body)

                f_dq(0)
                f_dq(1)
                for i in range(HT):
                    p = i % 2
                    p3 = i % 3

                    if i + 2 < HT:
                        f_dq(i + 2)

                    def f_te(tensor, i=i, p=p, p3=p3):
                        tk.wait(tensor, "t", t_wdq3[p3], t_x1, t_silu[p], t_hact[p])
                        last = None
                        for s in range(2):
                            for k in range(KT):
                                tensor.matmul(out=pg[p][s][:, :],
                                              lhsT=wg_raw[p3][:, k * 128:(k + 1) * 128],
                                              rhs=x1_bf[:, k * TS + s * 512: k * TS + (s + 1) * 512],
                                              start=(k == 0), stop=(k == KT - 1))
                            for k in range(KT):
                                last = tensor.matmul(out=pu[p][s][:, :],
                                                     lhsT=wu_raw[p3][:, k * 128:(k + 1) * 128],
                                                     rhs=x1_bf[:, k * TS + s * 512: k * TS + (s + 1) * 512],
                                                     start=(k == 0), stop=(k == KT - 1))
                        t_gmm[p] = tk.tag("tq", last)
                        t_gmm3[p3] = t_gmm[p]
                    blk.tensor(f_te)

                    def f_silu(scalar, i=i, p=p):
                        tk.wait(scalar, "s", t_gmm[p], t_consts, t_hact[p], t_silu[p])
                        if i == 0 and PRELOAD:
                            preload(scalar, 18)
                        last = None
                        for s in range(2):
                            last = scalar.activation(out=sg_act[p][:, s * 512:(s + 1) * 512],
                                                     in_=pg[p][s][:, :], func=AF.Silu,
                                                     scale=sgv[:, i:i + 1])
                        t_silu[p] = tk.tag("aq", last)
                    blk.scalar(f_silu)

                    if debug and i == 0:
                        def f_dbg(sync, p=p):
                            tk.wait(sync, "y", t_silu[p], t_wdq3[p % 3])
                            tk.tag("ddbg", sync.dma_start(out=dbg_sgv[:, :], in_=sgv[:]), 16)
                            tk.tag("ddbg", sync.dma_start(out=dbg_sg[:, :], in_=sg_act[p][:]), 16)
                            tk.tag("ddbg", sync.dma_start(out=dbg_wg[:, :], in_=wg_raw[p % 3][:]), 16)
                        blk.sync(f_dbg)

                    def f_hact(vector, i=i, p=p):
                        tk.wait(vector, "v", t_silu[p])
                        last = None
                        for s in range(2):
                            last = vector.scalar_tensor_tensor(
                                out=hact[:, i * TS + s * 512: i * TS + (s + 1) * 512],
                                in0=pu[p][s][:, :], scalar=suv[:, i:i + 1],
                                in1=sg_act[p][:, s * 512:(s + 1) * 512], op0=OP.mult, op1=OP.mult)
                        t_hact[p] = tk.tag("vq", last)
                    blk.vector(f_hact)

                    if i + 3 < HT:
                        f_wload(i + 3)

                if debug:
                    def f_dbg2(sync):
                        tk.wait(sync, "y", t_hact[0], t_hact[1])
                        tk.tag("ddbg", sync.dma_start(out=dbg_hact[:, :], in_=hact[:, 0:TS]), 16)
                    blk.sync(f_dbg2)

                t_hact_all = [t_hact[0], t_hact[1]]

            down = contextlib.ExitStack()
            with down:
                res2_sb = sdst.enter_context(nc.sbuf_tensor("res2", [128, KT * TS], BF16))
                wd_raw = [down.enter_context(nc.sbuf_tensor(f"wdr{p}", [128, HT * 128], BF16)) for p in range(2)]
                pd = [[down.enter_context(nc.psum_tensor(f"pd{p}{s}", [128, 512], F32)) for s in range(2)] for p in range(2)]

                t_dmm = [None, None]
                t_res2 = [None, None]

                def dn_wload(jd):
                    p2 = jd % 2

                    def body(gp, jd=jd, p2=p2):
                        tk.wait(gp, "g", t_dte[p2], t_wddq2[p2])
                        if jd < 2:
                            tk.wait(gp, "g", t_gmm[0], t_gmm[1], t_hact[0], t_hact[1],
                                    t_silu[0], t_silu[1])
                        t_wdc[p2] = tk.tag(f"dwdc{p2}", gp.dma_start(
                            out=wd_raw[p2][:],
                            in_=wd_d[jd * 128:(jd + 1) * 128, :]), 16)
                    blk.gpsimd(body)

                def dn_dq(jd):
                    p2 = jd % 2

                    def body(gp, jd=jd, p2=p2):
                        tk.wait(gp, "g", t_wdc[p2], t_dte[p2])
                        tk.tag("gq", gp.tensor_scalar(out=wd_raw[p2][:], in0=wd_raw[p2][:],
                                                      scalar1=-128.0, scalar2=1.0,
                                                      op0=OP.add, op1=OP.mult))
                        t_wddq2[p2] = ("gq", tk.n["gq"])
                    blk.gpsimd(body)

                dn_wload(0)
                dn_wload(1)
                dn_dq(0)
                dn_dq(1)

                for jd in range(KT):
                    p = jd % 2

                    def dn_te(tensor, jd=jd, p=p):
                        tk.wait(tensor, "t", t_wddq2[p], t_hact_all[0], t_hact_all[1],
                                t_res2[p])
                        last = None
                        for s in range(2):
                            for kh in range(HT):
                                last = tensor.matmul(out=pd[p][s][:, :],
                                                     lhsT=wd_raw[p][:, kh * 128:(kh + 1) * 128],
                                                     rhs=hact[:, kh * TS + s * 512: kh * TS + (s + 1) * 512],
                                                     start=(kh == 0), stop=(kh == HT - 1))
                        t_dmm[p] = tk.tag("tq", last)
                        t_dte[p] = t_dmm[p]
                    blk.tensor(dn_te)

                    def dn_epi(vector, jd=jd, p=p):
                        tk.wait(vector, "v", t_dmm[p])
                        last = None
                        for s in range(2):
                            last = vector.scalar_tensor_tensor(
                                out=res2_sb[:, jd * TS + s * 512: jd * TS + (s + 1) * 512],
                                in0=pd[p][s][:, :], scalar=sdv[:, jd:jd + 1],
                                in1=x1_bf[:, jd * TS + s * 512: jd * TS + (s + 1) * 512],
                                op0=OP.mult, op1=OP.add)
                        t_res2[p] = tk.tag("vq", last)
                    blk.vector(dn_epi)

                    if jd + 2 < KT:
                        dn_wload(jd + 2)
                        dn_dq(jd + 2)

                t_res2_all = [t_res2[0], t_res2[1]]

            ln2 = contextlib.ExitStack()
            with ln2:
                sq2 = [ln2.enter_context(nc.sbuf_tensor(f"sq2{p}", [128, TS], BF16)) for p in range(2)]
                srow2 = ln2.enter_context(nc.sbuf_tensor("srow2", [1, 3 * TS], F32))
                out_t = [ln2.enter_context(nc.sbuf_tensor(f"outt{p}", [128, TS], F32)) for p in range(2)]
                xtmp2 = ln2.enter_context(nc.sbuf_tensor("xtmp2", [128, 512], F32))
                ps2_s1 = [ln2.enter_context(nc.psum_tensor(f"p2s1{s}", [1, 512], F32)) for s in range(2)]
                ps2_s2 = [ln2.enter_context(nc.psum_tensor(f"p2s2{s}", [1, 512], F32)) for s in range(2)]
                ps2_r = [ln2.enter_context(nc.psum_tensor(f"p2r{s}", [128, 512], F32)) for s in range(2)]
                ps2_q = [ln2.enter_context(nc.psum_tensor(f"p2q{s}", [128, 512], F32)) for s in range(2)]

                t_sq2 = {}
                t_l2te = {}
                t_stat2 = None

                for k in range(KT):
                    p = k % 2

                    def l2_act(scalar, k=k, p=p):
                        tk.wait(scalar, "s", t_res2_all[0], t_res2_all[1],
                                t_l2te.get(k - 2), t_sq2.get(k - 2))
                        if k == 0 and PRELOAD:
                            preload(scalar, 6)
                        tk.tag("aq", scalar.activation(out=sq2[p][:], in_=res2_sb[:, k * TS:(k + 1) * TS],
                                                       func=AF.Square))
                        t_sq2[k] = ("aq", tk.n["aq"])
                    blk.scalar(l2_act)

                    def l2_te(tensor, k=k, p=p):
                        tk.wait(tensor, "t", t_sq2[k])
                        last = None
                        for s in range(2):
                            tensor.matmul(out=ps2_s1[s][:, :], lhsT=onescol[:, 0:1],
                                          rhs=res2_sb[:, k * TS + s * 512: k * TS + (s + 1) * 512],
                                          start=(k == 0), stop=(k == KT - 1))
                            last = tensor.matmul(out=ps2_s2[s][:, :], lhsT=onescol[:, 0:1],
                                                 rhs=sq2[p][:, s * 512:(s + 1) * 512],
                                                 start=(k == 0), stop=(k == KT - 1))
                        t_l2te[k] = tk.tag("tq", last)
                    blk.tensor(l2_te)
                t_stat2 = ("tq", tk.n["tq"])

                def l2_stats_vec(vector):
                    tk.wait(vector, "v", t_stat2)
                    for s in range(2):
                        tk.tag("vq", vector.tensor_scalar(out=srow2[:, s * 512:(s + 1) * 512],
                                                          in0=ps2_s1[s][:, :], scalar1=1.0 / D,
                                                          scalar2=None, op0=OP.mult))
                        tk.tag("vq", vector.tensor_scalar(out=srow2[:, TS + s * 512:TS + (s + 1) * 512],
                                                          in0=ps2_s2[s][:, :], scalar1=1.0 / D,
                                                          scalar2=None, op0=OP.mult))
                blk.vector(l2_stats_vec)
                t_mrow2 = ("vq", tk.n["vq"])

                def l2_stats_act(scalar):
                    tk.wait(scalar, "s", t_mrow2)
                    tk.tag("aq", scalar.activation(out=srow2[:, 2 * TS:3 * TS], in_=srow2[:, 0:TS],
                                                   func=AF.Square))
                blk.scalar(l2_stats_act)
                t_msq2 = ("aq", tk.n["aq"])

                def l2_sv2(vector):
                    tk.wait(vector, "v", t_msq2)
                    tk.tag("vq", vector.tensor_tensor(out=srow2[:, TS:2 * TS], in0=srow2[:, TS:2 * TS],
                                                      in1=srow2[:, 2 * TS:3 * TS], op=OP.subtract))
                blk.vector(l2_sv2)
                t_var2 = ("vq", tk.n["vq"])

                def l2_sa2(scalar):
                    tk.wait(scalar, "s", t_var2)
                    tk.tag("aq", scalar.activation(out=srow2[:, TS:2 * TS], in_=srow2[:, TS:2 * TS],
                                                   func=AF.Ln, bias=epsb[0:1, 0:1]))
                    tk.wait(scalar, "s", ("aq", tk.n["aq"]))
                    tk.tag("aq", scalar.activation(out=srow2[:, TS:2 * TS], in_=srow2[:, TS:2 * TS],
                                                   func=AF.Exp, scale=-0.5))
                blk.scalar(l2_sa2)
                t_rstd2 = ("aq", tk.n["aq"])

                def l2_sv3(vector):
                    tk.wait(vector, "v", t_rstd2)
                    tk.tag("vq", vector.tensor_tensor(out=srow2[:, 0:TS], in0=srow2[:, 0:TS],
                                                      in1=srow2[:, TS:2 * TS], op=OP.mult))
                blk.vector(l2_sv3)
                t_qrow2 = ("vq", tk.n["vq"])

                def l2_bc(tensor):
                    tk.wait(tensor, "t", t_qrow2)
                    last = None
                    for s in range(2):
                        tensor.matmul(out=ps2_r[s][:, :], lhsT=onesrow[0:1, :],
                                      rhs=srow2[0:1, TS + s * 512:TS + (s + 1) * 512],
                                      start=True, stop=True)
                        last = tensor.matmul(out=ps2_q[s][:, :], lhsT=onesrow[0:1, :],
                                             rhs=srow2[0:1, s * 512:(s + 1) * 512],
                                             start=True, stop=True)
                    tk.tag("tq", last)
                blk.tensor(l2_bc)
                t_bc2 = ("tq", tk.n["tq"])

                t_odma = [None, None]
                t_outw = [None, None]
                for k in range(KT):
                    p = k % 2

                    def l2_out(vector, k=k, p=p):
                        tk.wait(vector, "v", t_bc2, t_odma[p])
                        for s in range(2):
                            tk.wait(vector, "v", ("vq", tk.n["vq"]))
                            tk.tag("vq", vector.tensor_tensor(
                                out=xtmp2[:], in0=res2_sb[:, k * TS + s * 512: k * TS + (s + 1) * 512],
                                in1=ps2_r[s][:, :], op=OP.mult))
                            tk.wait(vector, "v", ("vq", tk.n["vq"]))
                            tk.tag("vq", vector.tensor_tensor(
                                out=out_t[p][:, s * 512:(s + 1) * 512], in0=xtmp2[:],
                                in1=ps2_q[s][:, :], op=OP.subtract))
                        t_outw[p] = ("vq", tk.n["vq"])
                    blk.vector(l2_out)

                    def l2_store(sync, k=k, p=p):
                        tk.wait(sync, "y", t_outw[p])
                        t_odma[p] = tk.tag(f"dot{p}", sync.dma_start(
                            out=out_d[k * 128:(k + 1) * 128, :], in_=out_t[p][:]), 16)
                    blk.sync(l2_store)

                if debug:
                    def dbg_l2(sync):
                        tk.wait(sync, "y", t_res2_all[0], t_res2_all[1])
                        tk.tag("ddbg", sync.dma_start(out=dbg_res2[:, :], in_=res2_sb[:]), 16)
                    blk.sync(dbg_l2)

                def fin(sync):
                    tk.wait(sync, "y", t_odma[0], t_odma[1])
                    if debug:
                        tk.wait(sync, "y", ("ddbg", tk.n.get("ddbg", 0)))
                blk.sync(fin)

    return nc


_NC_CACHE = None


def kernel(L0_attn_u, L0_attn_Wdt, L0_attn_bdt, L0_attn_Wb, L0_attn_bb,
           L0_attn_Wc, L0_attn_bc, L0_attn_a_raw, L0_attn_B_base, L0_attn_C_base,
           L0_attn_D, L0_ffn_wq_gate, L0_ffn_s_gate, L0_ffn_wq_up, L0_ffn_s_up,
           L0_ffn_wq_down, L0_ffn_s_down):
    global _NC_CACHE
    u = np.asarray(L0_attn_u, np.float32)[0]
    u_fm = np.ascontiguousarray(u.T)
    Wdt = np.asarray(L0_attn_Wdt, np.float32)
    Wb = np.asarray(L0_attn_Wb, np.float32)
    Wc = np.asarray(L0_attn_Wc, np.float32)
    bdt = np.asarray(L0_attn_bdt, np.float32)
    bb = np.asarray(L0_attn_bb, np.float32)
    bc = np.asarray(L0_attn_bc, np.float32)
    a_raw = np.asarray(L0_attn_a_raw, np.float32)
    B_base = np.asarray(L0_attn_B_base, np.float32)
    C_base = np.asarray(L0_attn_C_base, np.float32)
    Dsk = np.asarray(L0_attn_D, np.float32)
    wqg = np.asarray(L0_ffn_wq_gate, np.int32)
    wqu = np.asarray(L0_ffn_wq_up, np.int32)
    wqd = np.asarray(L0_ffn_wq_down, np.int32)
    sg = np.asarray(L0_ffn_s_gate, np.float32)
    su = np.asarray(L0_ffn_s_up, np.float32)
    sd = np.asarray(L0_ffn_s_down, np.float32)

    wg_t = np.ascontiguousarray(wqg.T.reshape(KT, 128, HT, 128).transpose(2, 1, 0, 3)).reshape(HT * 128, KT * 128)
    wu_t = np.ascontiguousarray(wqu.T.reshape(KT, 128, HT, 128).transpose(2, 1, 0, 3)).reshape(HT * 128, KT * 128)
    wd_t = np.ascontiguousarray(wqd.T.reshape(HT, 128, KT, 128).transpose(2, 1, 0, 3)).reshape(KT * 128, HT * 128)
    sg_a = np.ascontiguousarray(sg.reshape(HT, 128).T)
    su_a = np.ascontiguousarray(su.reshape(HT, 128).T)
    sd_a = np.ascontiguousarray(sd.reshape(KT, 128).T)

    in_maps = []
    for c in range(NC):
        sl = slice(c * DS, (c + 1) * DS)
        bdt_s, bb_s, bc_s = bdt[sl], bb[sl], bc[sl]
        bvec = np.stack([bdt_s[:128], bdt_s[128:], bb_s[:128], bb_s[128:],
                         bc_s[:128], bc_s[128:]], axis=1)
        araw_a = np.ascontiguousarray(a_raw[sl].reshape(2, 128, NST).transpose(1, 0, 2).reshape(128, 2 * NST))
        bbase_a = np.ascontiguousarray(B_base[sl].reshape(2, 128, NST).transpose(1, 0, 2).reshape(128, 2 * NST))
        cbase_a = np.ascontiguousarray(C_base[sl].reshape(2, 128, NST).transpose(1, 0, 2).reshape(128, 2 * NST))
        dvec = np.stack([Dsk[sl][:128], Dsk[sl][128:]], axis=1)
        yidx = np.zeros((128, KT), np.uint32)
        for k in range(KT):
            yidx[:, k] = c * NC * DS + k * 128 + np.arange(128, dtype=np.uint32)
        in_maps.append({
            "u_fm": u_fm,
            "u_ds": np.ascontiguousarray(u_fm[sl, :]),
            "u_tok": np.ascontiguousarray(u_fm[:, c * TS:(c + 1) * TS]),
            "wdt": np.ascontiguousarray(Wdt[:, sl]),
            "wb": np.ascontiguousarray(Wb[:, sl]),
            "wc": np.ascontiguousarray(Wc[:, sl]),
            "bvec": np.ascontiguousarray(bvec),
            "araw": araw_a, "bbase": bbase_a, "cbase": cbase_a,
            "dvec": np.ascontiguousarray(dvec),
            "sgv": sg_a, "suv": su_a, "sdv": sd_a,
            "wg_t": wg_t, "wu_t": wu_t, "wd_t": wd_t,
            "yidx": yidx,
            "ident": np.eye(128, dtype=ml_dtypes.bfloat16),
        })

    if _NC_CACHE is None:
        _NC_CACHE = build_nc()
    res = run_bass_kernel_spmd(_NC_CACHE, in_maps, core_ids=list(range(NC)))
    parts = [np.asarray(res.results[c]["out"], np.float32).T for c in range(NC)]
    out = np.concatenate(parts, axis=0)[None]
    return out
